# version 01
# baseline (speedup 1.0000x reference)
import math
from contextlib import ExitStack

import numpy as np
import ml_dtypes

import concourse.bass as bass
import concourse.mybir as mybir
from concourse.bass_utils import run_bass_kernel_spmd

F32 = mybir.dt.float32
BF16 = mybir.dt.bfloat16
I32 = mybir.dt.int32
AF = mybir.ActivationFunctionType
ALU = mybir.AluOpType
AX = mybir.AxisListType

EPS = 1e-5
THETA = 10000.0
CEXP = 24.0


class Cfg:
    def __init__(self, D=4096, S=4096, Q=1024, H=32, KVH=8, LAT=1024, NHID=11008,
                 ncores=8, B=2):
        self.D, self.S, self.Q, self.H, self.KVH, self.LAT, self.NHID = \
            D, S, Q, H, KVH, LAT, NHID
        self.ncores, self.B = ncores, B
        self.HD = 128
        self.DT = D // 128
        self.LT = LAT // 128
        self.KT = S // 128
        self.QT = Q // 128
        self.NHT = NHID // 128
        self.DR = D // 128
        self.HR = NHID // 128
        self.RC = min(1024, S)
        self.NRC = S // self.RC
        self.G = H // KVH
        self.SCALE = 1.0 / math.sqrt(self.HD)
        self.cpb = ncores // B


def _splits(total, step):
    return [(o, min(step, total - o)) for o in range(0, total, step)]


class Ch:

    def __init__(self, nc, stack, name):
        self.s = stack.enter_context(nc.semaphore(name))
        self.n = 0

    def inc(self, instr, k=1):
        instr.then_inc(self.s, k)
        self.n += k
        return self.n

    def wait(self, eng, n=None):
        eng.wait_ge(self.s, self.n if n is None else n)


def build_program(c: Cfg):
    nc = bass.Bass("TRN2", target_bir_lowering=False)
    D, S, Q, H, KVH, LAT, NHID = c.D, c.S, c.Q, c.H, c.KVH, c.LAT, c.NHID
    DT, LT, KT, QT, NHT, DR, HR = c.DT, c.LT, c.KT, c.QT, c.NHT, c.DR, c.HR
    RC, NRC, G = c.RC, c.NRC, c.G

    dtt = nc.dram_tensor
    xf = dtt("x_full", [S, D], F32, kind="ExternalInput").ap()
    xq = dtt("x_q", [Q, D], F32, kind="ExternalInput").ap()
    w_q = dtt("w_q", [D, H * 128], F32, kind="ExternalInput").ap()
    w_k = dtt("w_k", [D, KVH * 128], F32, kind="ExternalInput").ap()
    w_v = dtt("w_v", [D, KVH * 128], F32, kind="ExternalInput").ap()
    w_o = dtt("w_o", [H * 128, D], F32, kind="ExternalInput").ap()
    w_lq = dtt("w_lq", [D, LAT], F32, kind="ExternalInput").ap()
    w_lk = dtt("w_lk", [LAT, H * 128], F32, kind="ExternalInput").ap()
    w_lv = dtt("w_lv", [D, H * 128], F32, kind="ExternalInput").ap()
    w_lo = dtt("w_lo", [H * 128, D], F32, kind="ExternalInput").ap()
    pk1 = dtt("pk1", [NHID * DR, 64], I32, kind="ExternalInput").ap()
    s1 = dtt("s1", [NHID * DR * 2], F32, kind="ExternalInput").ap()
    pk2 = dtt("pk2", [D * HR, 64], I32, kind="ExternalInput").ap()
    s2 = dtt("s2", [D * HR * 2], F32, kind="ExternalInput").ap()
    pk3 = dtt("pk3", [NHID * DR, 64], I32, kind="ExternalInput").ap()
    s3 = dtt("s3", [NHID * DR * 2], F32, kind="ExternalInput").ap()
    cosk = dtt("cosk", [128, S], F32, kind="ExternalInput").ap()
    sinmk = dtt("sinmk", [128, S], F32, kind="ExternalInput").ap()
    cosq = dtt("cosq", [128, Q], F32, kind="ExternalInput").ap()
    sinmq = dtt("sinmq", [128, Q], F32, kind="ExternalInput").ap()
    maskT = dtt("maskT", [S, Q], BF16, kind="ExternalInput").ap()
    identb = dtt("identb", [128, 128], BF16, kind="ExternalInput").ap()
    onesf = dtt("onesf", [1, 128], F32, kind="ExternalInput").ap()
    onesb = dtt("onesb", [128, 1], BF16, kind="ExternalInput").ap()
    constv = dtt("constv", [128, 4], F32, kind="ExternalInput").ap()
    out_d = dtt("out", [Q, D], F32, kind="ExternalOutput").ap()

    k_sc = dtt("k_sc", [KVH * 128, S], BF16, kind="Internal").ap()
    v_sc = dtt("v_sc", [KVH * S, 128], BF16, kind="Internal").ap()
    lk_sc = dtt("lk_sc", [H * 128, S], BF16, kind="Internal").ap()
    lv_sc = dtt("lv_sc", [H * S, 128], BF16, kind="Internal").ap()
    q_sc = dtt("q_sc", [H * 128, Q], BF16, kind="Internal").ap()
    ctx_sc = dtt("ctx_sc", [2 * H * 128, Q], BF16, kind="Internal").ap()
    x1a_sc = dtt("x1a_sc", [Q, D], F32, kind="Internal").ap()
    y_sc = dtt("y_sc", [Q, D], F32, kind="Internal").ap()
    hact_sc = dtt("hact_sc", [NHT * 128, Q], BF16, kind="Internal").ap()

    sy, g, t, v, s = nc.sync, nc.gpsimd, nc.tensor, nc.vector, nc.scalar

    with ExitStack() as st0:
        idt = st0.enter_context(nc.sbuf_tensor("idt", [128, 128], BF16))
        cb = st0.enter_context(nc.sbuf_tensor("cb", [128, 4], F32))
        onef = st0.enter_context(nc.sbuf_tensor("onef", [1, 128], F32))
        oneb = st0.enter_context(nc.sbuf_tensor("oneb", [128, 1], BF16))

        cv = Ch(nc, st0, "cv")
        cs = Ch(nc, st0, "cs")
        ct = Ch(nc, st0, "ct")
        dch = [Ch(nc, st0, f"d{i}") for i in range(24)]
        gch = [Ch(nc, st0, f"g{i}") for i in range(4)]

        blk = st0.enter_context(nc.Block())

        def vop(fn, *a, **kw):
            v.wait_ge(cv.s, cv.n)
            return cv.inc(fn(*a, **kw))

        def sop(fn, *a, **kw):
            s.wait_ge(cs.s, cs.n)
            return cs.inc(fn(*a, **kw))

        def barrier():
            for ch_ in dch:
                if ch_.n:
                    ch_.wait(sy)
            for ch_ in gch:
                if ch_.n:
                    ch_.wait(g)
            nc.all_engine_barrier()

        ch0 = dch[0]
        sy.dma_start(idt[:, :], identb).then_inc(ch0.s, 16); ch0.n += 16
        sy.dma_start(onef[:, :], onesf).then_inc(ch0.s, 16); ch0.n += 16
        sy.dma_start(oneb[:, :], onesb).then_inc(ch0.s, 16); ch0.n += 16
        sy.dma_start(cb[:, :], constv).then_inc(ch0.s, 16); ch0.n += 16
        ch0.wait(t)
        ch0.wait(v)
        ch0.wait(s)

        def transpose_to(dst3, n_tiles, src_ap_fn, ptrb, ptrb_tok, src_cv_tok):
            if src_cv_tok:
                cv.wait(t, src_cv_tok)
            last = 0
            for gi, (g0, gl) in enumerate(_splits(n_tiles, 8)):
                bsel = gi % 2
                cv.wait(t, ptrb_tok[bsel])
                for j in range(gl):
                    ins = t.transpose(ptrb[bsel][:, j * 128:(j + 1) * 128],
                                      src_ap_fn(g0 + j), idt[:, :])
                tokt = ct.inc(ins)
                ct.wait(v, tokt)
                srcv = ptrb[bsel].ap().rearrange("p (j k) -> p j k", k=128)[:, 0:gl, :]
                ptrb_tok[bsel] = vop(v.tensor_copy, dst3[:, g0:g0 + gl, :], srcv)
                last = ptrb_tok[bsel]
            return last

        def ln_rows(src_ap, xt, sq, lnbf, stats, ld_ch, war_cv, war_cs, extra=None):
            cv.wait(sy, war_cv)
            cs.wait(sy, war_cs)
            sy.dma_start(xt[:, :], src_ap).then_inc(ld_ch.s, 16)
            ld_ch.n += 16
            if extra is not None:
                extra()
            nsum, ssq, m2, varv, rstd = stats
            ld_ch.wait(s)
            s.wait_ge(cv.s, cv.n)
            toksq = sop(s.activation, sq[:, :], xt[:, :], AF.Square,
                        accum_out=ssq[:, :])
            ld_ch.wait(v)
            vop(v.tensor_reduce, nsum[:, :], xt[:, :], AX.X, ALU.add)
            vop(v.tensor_scalar_mul, nsum[:, :], nsum[:, :], -1.0 / D)
            vop(v.tensor_tensor, m2[:, :], nsum[:, :], nsum[:, :], ALU.mult)
            cs.wait(v, toksq)
            vop(v.tensor_scalar_mul, varv[:, :], ssq[:, :], 1.0 / D)
            tokv = vop(v.tensor_tensor, varv[:, :], varv[:, :], m2[:, :],
                       ALU.subtract)
            cv.wait(s, tokv)
            sop(s.activation, varv[:, :], varv[:, :], AF.Ln, bias=cb[:, 0:1])
            toks = sop(s.activation, rstd[:, :], varv[:, :], AF.Exp, scale=cb[:, 1:2])
            cs.wait(v, toks)
            tokl = vop(v.tensor_scalar, lnbf[:, :], xt[:, :],
                       nsum[:, 0:1], rstd[:, 0:1], ALU.add, ALU.mult)
            return tokl, toksq

        with ExitStack() as stA:
            sbA = lambda n, sh, d: stA.enter_context(nc.sbuf_tensor(n, sh, d))
            psA = lambda n, sh, d: stA.enter_context(nc.psum_tensor(n, sh, d))
            lnT = sbA("lnT", [128, DT * RC], BF16)
            lnT3 = lnT.ap().rearrange("p (d r) -> p d r", d=DT)
            lqT = sbA("lqT", [128, LT * RC], BF16)
            lqT3 = lqT.ap().rearrange("p (l r) -> p l r", l=LT)
            xt = [sbA(f"xt{i}", [128, D], F32) for i in range(2)]
            sqj = sbA("sqj", [128, D], F32)
            lnbf = sbA("lnbf", [128, D], BF16)
            stats = [sbA(f"st{i}", [128, 1], F32) for i in range(5)]
            wt_bf = [sbA(f"wtbf{i}", [128, DT * 128], BF16) for i in range(2)]
            wt3 = [w.ap().rearrange("p (d o) -> p d o", d=DT) for w in wt_bf]
            wlk_bf = [sbA(f"wlkbf{i}", [128, LT * 128], BF16) for i in range(2)]
            wlk3 = [w.ap().rearrange("p (l o) -> p l o", l=LT) for w in wlk_bf]
            cosc = sbA("cosc", [128, RC], F32)
            sinc = sbA("sinc", [128, RC], F32)
            rt1 = sbA("rt1", [128, RC], F32)
            rt2 = sbA("rt2", [128, RC], F32)
            obf = [sbA(f"obf{i}", [128, RC], BF16) for i in range(3)]
            vtb = [sbA(f"vtb{i}", [128, RC], BF16) for i in range(2)]
            ptrb = [psA(f"ptrb{i}", [128, 8 * 128], BF16) for i in range(2)]
            pso = [psA(f"pso{i}", [128, min(512, RC)], F32) for i in range(2)]

            ch1 = dch[1]

            xf3 = xf.rearrange("(a p) d -> a p d", p=128)
            xq3 = xq.rearrange("(a p) d -> a p d", p=128)
            wq_r = w_q.rearrange("(a b) c -> b a c", b=128)
            wk_r = w_k.rearrange("(a b) c -> b a c", b=128)
            wv_r = w_v.rearrange("(a b) c -> b a c", b=128)
            wlq_r = w_lq.rearrange("(a b) c -> b a c", b=128)
            wlv_r = w_lv.rearrange("(a b) c -> b a c", b=128)
            wlk_r = w_lk.rearrange("(a b) c -> b a c", b=128)

            ptrb_tok = [0, 0]

            def proj_chunk(is_q, rc_idx, nrows):
                r0 = rc_idx * RC
                nrt = nrows // 128
                if is_q:
                    sy.dma_start(cosc[:, 0:Q], cosq).then_inc(ch1.s, 16)
                    sy.dma_start(sinc[:, 0:Q], sinmq).then_inc(ch1.s, 16)
                else:
                    sy.dma_start(cosc[:, :], cosk[:, r0:r0 + RC]) \
                        .then_inc(ch1.s, 16)
                    sy.dma_start(sinc[:, :], sinmk[:, r0:r0 + RC]) \
                        .then_inc(ch1.s, 16)
                ch1.n += 32
                ch1.wait(v)
                xt_war = [(0, 0), (0, 0)]
                for ti in range(nrt):
                    bs = ti % 2
                    src = (xq3 if is_q else xf3)[(r0 // 128) + ti, :, :]
                    tokl, toksq = ln_rows(src, xt[bs], sqj, lnbf, stats,
                                          dch[2 + bs], *xt_war[bs])
                    xt_war[bs] = (tokl, toksq)
                    transpose_to(lnT3[:, :, ti * 128:(ti + 1) * 128], DT,
                                 lambda dd: lnbf[:, dd * 128:(dd + 1) * 128],
                                 ptrb, ptrb_tok, tokl)
                lnT_ready = cv.n

                if is_q:
                    wlist = [("q", wq_r, hh) for hh in range(H)]
                else:
                    wlist = ([("k", wk_r, hh) for hh in range(KVH)]
                             + [("v", wv_r, hh) for hh in range(KVH)]
                             + [("lq", wlq_r, oc) for oc in range(LT)]
                             + [("lv", wlv_r, hh) for hh in range(H)])
                cos_t, sin_t = cosc, sinc
                cpos0 = 0
                nfs = _splits(nrows, 512)

                wt_last_mm = [0, 0]
                pso_tok = [0, 0]
                obf_rd = [None, None, None]
                vtb_rd = [None, None]

                def wait_reader(eng, rd):
                    if rd is None:
                        return
                    kind, a, b = rd
                    if kind == "cv":
                        cv.wait(eng, a)
                    elif kind == "ct":
                        ct.wait(eng, a)
                    else:
                        a.wait(eng, b)

                def emit_proj(wi, ty, wr, oc, lq_ready):
                    nonlocal ptrb_tok
                    ws = wi % 2
                    gc = gch[ws]
                    ct.wait(g, wt_last_mm[ws])
                    if ty == "lk":
                        g.dma_start(wlk_bf[ws][:, :],
                                    wr[:, :, oc * 128:(oc + 1) * 128]) \
                            .then_inc(gc.s, 16)
                        lhs3, nlhs, rhs3, rhs_rdy = wlk3[ws], LT, lqT3, lq_ready
                    else:
                        g.dma_start(wt_bf[ws][:, :],
                                    wr[:, :, oc * 128:(oc + 1) * 128]) \
                            .then_inc(gc.s, 16)
                        lhs3, nlhs, rhs3, rhs_rdy = wt3[ws], DT, lnT3, lnT_ready
                    gc.n += 16
                    ob_i = wi % 3
                    ob = obf[ob_i]
                    rope_toks = []
                    for nf, (no, nl) in enumerate(nfs):
                        po = (wi * len(nfs) + nf) % 2
                        cv.wait(t, pso_tok[po])
                        gc.wait(t)
                        t.wait_ge(cv.s, rhs_rdy)
                        for dd in range(nlhs):
                            ins = t.matmul(pso[po][:, 0:nl], lhs3[:, dd, :],
                                           rhs3[:, dd, no:no + nl],
                                           start=(dd == 0), stop=(dd == nlhs - 1))
                        mm_tok = ct.inc(ins)
                        wt_last_mm[ws] = mm_tok
                        rows = slice(no, no + nl)
                        ct.wait(v, mm_tok)
                        if nf == 0:
                            wait_reader(v, obf_rd[ob_i])
                        if ty in ("k", "lk", "q"):
                            cp = cos_t[:, cpos0 + no:cpos0 + no + nl]
                            sp = sin_t[:, cpos0 + no:cpos0 + no + nl]
                            vop(v.tensor_tensor, rt1[:, rows], pso[po][:, 0:nl],
                                cp, ALU.mult)
                            vop(v.tensor_tensor, rt2[0:64, rows],
                                pso[po][64:128, 0:nl], sp[0:64, :], ALU.mult)
                            vop(v.tensor_tensor, rt2[64:128, rows],
                                pso[po][0:64, 0:nl], sp[64:128, :], ALU.mult)
                            tokr = vop(v.tensor_tensor, ob[:, rows],
                                       rt1[:, rows], rt2[:, rows], ALU.add)
                        else:
                            tokr = vop(v.tensor_copy, ob[:, rows],
                                       pso[po][:, 0:nl])
                        pso_tok[po] = tokr
                        rope_toks.append(tokr)
                    if ty in ("k", "lk", "q"):
                        c_st = dch[6 + ob_i]
                        cv.wait(sy, rope_toks[-1])
                        dst = {"k": k_sc, "lk": lk_sc, "q": q_sc}[ty]
                        if ty == "q":
                            tgt = dst[oc * 128:(oc + 1) * 128, 0:nrows]
                        else:
                            tgt = dst[oc * 128:(oc + 1) * 128, r0:r0 + nrows]
                        sy.dma_start(tgt, ob[:, 0:nrows]).then_inc(c_st.s, 16)
                        c_st.n += 16
                        obf_rd[ob_i] = ("dma", c_st, c_st.n)
                    elif ty == "lq":
                        pass
                    if ty in ("v", "lv"):
                        vs = wi % 2
                        wait_reader(v, vtb_rd[vs])
                        vt3 = vtb[vs].ap().rearrange("p (j k) -> p j k", k=128)
                        tokT = transpose_to(vt3, nrt,
                                            lambda j: ob[:, j * 128:(j + 1) * 128],
                                            ptrb, ptrb_tok, rope_toks[-1])
                        obf_rd[ob_i] = ("ct", ct.n, None)
                        c_st = dch[9 + vs]
                        cv.wait(sy, tokT)
                        dst = v_sc if ty == "v" else lv_sc
                        dst_r = dst[oc * S + r0:oc * S + r0 + nrows, :] \
                            .rearrange("(j p) h -> p j h", p=128)
                        sy.dma_start(dst_r, vt3[:, 0:nrt, :]).then_inc(c_st.s, 16)
                        c_st.n += 16
                        vtb_rd[vs] = ("dma", c_st, c_st.n)

                def emit_lq(wi, oc):
                    ws = wi % 2
                    gc = gch[ws]
                    ct.wait(g, wt_last_mm[ws])
                    g.dma_start(wt_bf[ws][:, :],
                                wlq_r[:, :, oc * 128:(oc + 1) * 128]) \
                        .then_inc(gc.s, 16)
                    gc.n += 16
                    for nf, (no, nl) in enumerate(nfs):
                        po = (wi * len(nfs) + nf) % 2
                        cv.wait(t, pso_tok[po])
                        gc.wait(t)
                        t.wait_ge(cv.s, lnT_ready)
                        for dd in range(DT):
                            ins = t.matmul(pso[po][:, 0:nl], wt3[ws][:, dd, :],
                                           lnT3[:, dd, no:no + nl],
                                           start=(dd == 0), stop=(dd == DT - 1))
                        mm_tok = ct.inc(ins)
                        wt_last_mm[ws] = mm_tok
                        ct.wait(v, mm_tok)
                        pso_tok[po] = vop(v.tensor_copy,
                                          lqT3[:, oc, no:no + nl],
                                          pso[po][:, 0:nl])

                wi = 0
                for ty, wr, oc in wlist:
                    if ty == "lq":
                        emit_lq(wi, oc)
                    else:
                        emit_proj(wi, ty, wr, oc, 0)
                    wi += 1
                if not is_q:
                    lq_ready = cv.n
                    for oc in range(H):
                        emit_proj(wi, "lk", wlk_r, oc, lq_ready)
                        wi += 1
                barrier()

            for rc_idx in range(NRC):
                proj_chunk(False, rc_idx, RC)
            proj_chunk(True, 0, Q)
            barrier()

        with ExitStack() as stB:
            sbB = lambda n, sh, d: stB.enter_context(nc.sbuf_tensor(n, sh, d))
            psB = lambda n, sh, d: stB.enter_context(nc.psum_tensor(n, sh, d))
            maskb = sbB("maskb", [128, KT * Q], BF16)
            mask3 = maskb.ap().rearrange("p (k q) -> p k q", k=KT)
            kbuf = [sbB(f"kb{i}", [128, S], BF16) for i in range(2)]
            vbuf = [sbB(f"vb{i}", [128, KT * 128], BF16) for i in range(2)]
            lkbuf = [sbB(f"lkb{i}", [128, S], BF16) for i in range(2)]
            lvbuf = [sbB(f"lvb{i}", [128, KT * 128], BF16) for i in range(2)]
            qbuf = [sbB(f"qb{i}", [128, Q], BF16) for i in range(2)]
            pmb = [sbB(f"pm{i}", [128, Q], BF16) for i in range(2)]
            ctxb = [sbB(f"ctx{i}", [128, Q], BF16) for i in range(2)]
            recipb = sbB("recipb", [1, Q], F32)
            rbb = sbB("rbb", [128, Q], F32)
            ps_s = [psB(f"pss{i}", [128, Q], F32) for i in range(2)]
            ps_ctx = psB("psctx", [128, Q], F32)
            ps_sum = psB("pssum", [1, Q], F32)

            chm = dch[11]
            sy.dma_start(mask3, maskT.rearrange("(k p) q -> p k q", p=128)) \
                .then_inc(chm.s, 16)
            chm.n += 16
            chm.wait(v)
            chm.wait(s)

            dKV = [dch[12], dch[13]]
            dLTc = [dch[14], dch[15]]
            dCX = [dch[16], dch[17]]
            kv_tok = {}
            done_tok = {}

            def load_kv(hg):
                slot = hg % 2
                ct.wait(sy, done_tok.get(("kv", hg - 2), 0))
                ch = dKV[slot]
                sy.dma_start(kbuf[slot][:, :],
                             k_sc[hg * 128:(hg + 1) * 128, :]).then_inc(ch.s, 16)
                sy.dma_start(vbuf[slot].ap().rearrange("p (k j) -> p k j", j=128),
                             v_sc[hg * S:(hg + 1) * S, :]
                             .rearrange("(k p) j -> p k j", p=128)) \
                    .then_inc(ch.s, 16)
                ch.n += 32
                kv_tok[("kv", hg)] = ch.n

            def load_lat(h):
                slot = h % 2
                ct.wait(sy, done_tok.get(("lt", h - 2), 0))
                ch = dLTc[slot]
                sy.dma_start(lkbuf[slot][:, :],
                             lk_sc[h * 128:(h + 1) * 128, :]).then_inc(ch.s, 16)
                sy.dma_start(lvbuf[slot].ap().rearrange("p (k j) -> p k j", j=128),
                             lv_sc[h * S:(h + 1) * S, :]
                             .rearrange("(k p) j -> p k j", p=128)) \
                    .then_inc(ch.s, 16)
                sy.dma_start(qbuf[slot][:, :],
                             q_sc[h * 128:(h + 1) * 128, :]).then_inc(ch.s, 16)
                ch.n += 48
                kv_tok[("lt", h)] = ch.n

            load_kv(0)
            load_lat(0)

            exp_tok = [0] * KT
            pm_tok = [0] * KT
            pv_tok = [0] * KT
            ctx_dma = [0, 0]
            tokRB_prev = 0
            qsp = _splits(Q, 512)

            for h in range(H):
                if h + 1 < H:
                    if (h + 1) % G == 0:
                        load_kv((h + 1) // G)
                    load_lat(h + 1)
                for a in range(2):
                    lslot = h % 2
                    if a == 0:
                        kvs = (h // G) % 2
                        kb_, vb_ = kbuf[kvs], vbuf[kvs]
                        dKV[kvs].wait(t, kv_tok[("kv", h // G)])
                    else:
                        kb_, vb_ = lkbuf[lslot], lvbuf[lslot]
                    dLTc[lslot].wait(t, kv_tok[("lt", h)])
                    qb_ = qbuf[lslot]
                    vb3 = vb_.ap().rearrange("p (k j) -> p k j", j=128)
                    for kt in range(KT + 1):
                        if kt < KT:
                            sl = kt % 2
                            cs.wait(t, exp_tok[(kt - 2) % KT])
                            if kt == 0:
                                cv.wait(t, tokRB_prev)
                            for qo, ql in qsp:
                                ins = t.matmul(ps_s[sl][:, qo:qo + ql],
                                               kb_[:, kt * 128:(kt + 1) * 128],
                                               qb_[:, qo:qo + ql],
                                               start=True, stop=True)
                            tokS = ct.inc(ins)
                            ct.wait(s, tokS)
                            ct.wait(s, pv_tok[(kt - 2) % KT])
                            s.wait_ge(cs.s, cs.n)
                            exp_tok[kt] = cs.inc(s.activation(
                                pmb[sl][:, :], ps_s[sl][:, :], AF.Exp,
                                scale=cb[:, 2:3], bias=cb[:, 3:4]))
                            cs.wait(v, exp_tok[kt])
                            pm_tok[kt] = vop(v.tensor_tensor, pmb[sl][:, :],
                                             pmb[sl][:, :], mask3[:, kt, :],
                                             ALU.mult)
                        if kt >= 1:
                            k2 = kt - 1
                            sl2 = k2 % 2
                            cv.wait(t, pm_tok[k2])
                            for qo, ql in qsp:
                                t.matmul(ps_sum[0:1, qo:qo + ql], oneb[:, :],
                                         pmb[sl2][:, qo:qo + ql],
                                         start=(k2 == 0), stop=(k2 == KT - 1))
                                ins = t.matmul(ps_ctx[:, qo:qo + ql],
                                               vb3[:, k2, :],
                                               pmb[sl2][:, qo:qo + ql],
                                               start=(k2 == 0), stop=(k2 == KT - 1))
                            pv_tok[k2] = ct.inc(ins)
                    ct.wait(v, pv_tok[KT - 1])
                    tokR = vop(v.reciprocal, recipb[:, :], ps_sum[:, :])
                    cv.wait(t, tokR)
                    cs.wait(t, exp_tok[KT - 1])
                    for qo, ql in qsp:
                        ins = t.matmul(ps_s[KT % 2][:, qo:qo + ql], onef[:, :],
                                       recipb[0:1, qo:qo + ql],
                                       start=True, stop=True)
                    tokB = ct.inc(ins)
                    ct.wait(v, tokB)
                    tokRB = vop(v.tensor_copy, rbb[:, :], ps_s[KT % 2][:, :])
                    tokRB_prev = tokRB
                    cslot = a
                    dCX[cslot].wait(v, ctx_dma[cslot])
                    tokC = vop(v.tensor_tensor, ctxb[cslot][:, :], ps_ctx[:, :],
                               rbb[:, :], ALU.mult)
                    cv.wait(sy, tokC)
                    sy.dma_start(
                        ctx_sc[(a * H + h) * 128:(a * H + h + 1) * 128, :],
                        ctxb[cslot][:, :]).then_inc(dCX[cslot].s, 16)
                    dCX[cslot].n += 16
                    ctx_dma[cslot] = dCX[cslot].n
                    cv.wait(t, tokC)
                    if a == 1:
                        done_tok[("kv", h // G)] = ct.n
                        done_tok[("lt", h)] = ct.n
            barrier()

        with ExitStack() as stO:
            sbO = lambda n, sh, d: stO.enter_context(nc.sbuf_tensor(n, sh, d))
            psO = lambda n, sh, d: stO.enter_context(nc.psum_tensor(n, sh, d))
            ctxa = sbO("ctxa", [128, H * Q], BF16)
            ctxa3 = ctxa.ap().rearrange("p (h q) -> p h q", h=H)
            wob = [sbO(f"wob{i}", [128, H * 512], BF16) for i in range(2)]
            wob3 = [w.ap().rearrange("p (h d) -> p h d", h=H) for w in wob]
            o_bf = [sbO(f"o_bf{i}", [128, 512], F32) for i in range(2)]
            x1t = [sbO(f"x1t{i}", [128, 512], F32) for i in range(2)]
            ps_op = [psO(f"psop{i}", [128, 512], F32) for i in range(2)]

            dsp = _splits(D, 512)
            for p_i, (wsrc, dst_sc) in enumerate([(w_o, x1a_sc), (w_lo, y_sc)]):
                wsrc_r = wsrc.rearrange("(a b) c -> b a c", b=128)
                cca = dch[18]
                sy.dma_start(ctxa3,
                             ctx_sc[p_i * H * 128:(p_i + 1) * H * 128, :]
                             .rearrange("(h p) q -> p h q", p=128)) \
                    .then_inc(cca.s, 16)
                cca.n += 16
                cca.wait(t)
                wo_mm = [0, 0]
                ps_tok = [0, 0]
                dXc = [dch[12], dch[13]]
                dOc = [dch[19], dch[20]]
                x1_rd = [0, 0]
                for di, (do, dl) in enumerate(dsp):
                    wslot = di % 2
                    gc = gch[2 + wslot]
                    ct.wait(g, wo_mm[wslot])
                    g.dma_start(wob[wslot][:, 0:H * dl],
                                wsrc_r[:, :, do:do + dl]).then_inc(gc.s, 16)
                    gc.n += 16
                    for qsub in range(QT):
                        pslot = qsub % 2
                        bsel = (di * QT + qsub) % 2
                        if p_i == 1:
                            cv.wait(sy, x1_rd[bsel])
                            sy.dma_start(
                                x1t[bsel][:, 0:dl],
                                x1a_sc[qsub * 128:(qsub + 1) * 128, do:do + dl]) \
                                .then_inc(dXc[bsel].s, 16)
                            dXc[bsel].n += 16
                        cv.wait(t, ps_tok[pslot])
                        gc.wait(t)
                        for ht in range(H):
                            ins = t.matmul(
                                ps_op[pslot][:, 0:dl],
                                ctxa3[:, ht, qsub * 128:(qsub + 1) * 128],
                                wob3[wslot][:, ht, 0:dl],
                                start=(ht == 0), stop=(ht == H - 1))
                        tokM = ct.inc(ins)
                        wo_mm[wslot] = tokM
                        ct.wait(v, tokM)
                        dOc[bsel].wait(v)
                        if p_i == 0:
                            tokE = vop(v.tensor_copy, o_bf[bsel][:, 0:dl],
                                       ps_op[pslot][:, 0:dl])
                        else:
                            dXc[bsel].wait(v)
                            tokE = vop(v.tensor_tensor, o_bf[bsel][:, 0:dl],
                                       ps_op[pslot][:, 0:dl],
                                       x1t[bsel][:, 0:dl], ALU.add)
                            x1_rd[bsel] = tokE
                        ps_tok[pslot] = tokE
                        cv.wait(sy, tokE)
                        sy.dma_start(
                            dst_sc[qsub * 128:(qsub + 1) * 128, do:do + dl],
                            o_bf[bsel][:, 0:dl]).then_inc(dOc[bsel].s, 16)
                        dOc[bsel].n += 16
                barrier()

        with ExitStack() as stF:
            sbF = lambda n, sh, d: stF.enter_context(nc.sbuf_tensor(n, sh, d))
            psF = lambda n, sh, d: stF.enter_context(nc.psum_tensor(n, sh, d))
            ln2T = sbF("ln2T", [128, DT * Q], BF16)
            ln2T3 = ln2T.ap().rearrange("p (d r) -> p d r", d=DT)
            ptrb = [psF(f"ptrbF{i}", [128, 8 * 128], BF16) for i in range(2)]
            with ExitStack() as stF1:
                sbF1 = lambda n, sh, d: stF1.enter_context(nc.sbuf_tensor(n, sh, d))
                xt = [sbF1(f"xtF{i}", [128, D], F32) for i in range(2)]
                yb = [sbF1(f"ybF{i}", [128, D], F32) for i in range(2)]
                sqj = sbF1("sqjF", [128, D], F32)
                lnbf = sbF1("lnbfF", [128, D], BF16)
                stats = [sbF1(f"stF{i}", [128, 1], F32) for i in range(5)]
                ptrb_tok = [0, 0]
                yw = [(0, 0), (0, 0)]
                xq3b = xq.rearrange("(a p) d -> a p d", p=128)
                dYs = [dch[4], dch[5]]
                for qsub in range(QT):
                    bs = qsub % 2
                    ch = dch[2 + bs]

                    def extra():
                        dYs[bs].wait(sy)
                        sy.dma_start(yb[bs][:, :],
                                     y_sc[qsub * 128:(qsub + 1) * 128, :]) \
                            .then_inc(ch.s, 16)
                        ch.n += 16

                    tok_add = [0]
                    cv.wait(sy, yw[bs][0])
                    cs.wait(sy, yw[bs][1])
                    sy.dma_start(xt[bs][:, :], xq3b[qsub, :, :]) \
                        .then_inc(ch.s, 16)
                    ch.n += 16
                    extra()
                    ch.wait(v)
                    toky = vop(v.tensor_tensor, yb[bs][:, :], yb[bs][:, :],
                               xt[bs][:, :], ALU.add)
                    cv.wait(sy, toky)
                    sy.dma_start(y_sc[qsub * 128:(qsub + 1) * 128, :],
                                 yb[bs][:, :]).then_inc(dYs[bs].s, 16)
                    dYs[bs].n += 16
                    nsum, ssq, m2, varv, rstd = stats
                    cv.wait(s, toky)
                    s.wait_ge(cv.s, cv.n)
                    toksq = sop(s.activation, sqj[:, :], yb[bs][:, :], AF.Square,
                                accum_out=ssq[:, :])
                    vop(v.tensor_reduce, nsum[:, :], yb[bs][:, :], AX.X, ALU.add)
                    vop(v.tensor_scalar_mul, nsum[:, :], nsum[:, :], -1.0 / D)
                    vop(v.tensor_tensor, m2[:, :], nsum[:, :], nsum[:, :],
                        ALU.mult)
                    cs.wait(v, toksq)
                    vop(v.tensor_scalar_mul, varv[:, :], ssq[:, :], 1.0 / D)
                    tokv = vop(v.tensor_tensor, varv[:, :], varv[:, :], m2[:, :],
                               ALU.subtract)
                    cv.wait(s, tokv)
                    sop(s.activation, varv[:, :], varv[:, :], AF.Ln, bias=cb[:, 0:1])
                    toks = sop(s.activation, rstd[:, :], varv[:, :], AF.Exp,
                               scale=cb[:, 1:2])
                    cs.wait(v, toks)
                    tokl = vop(v.tensor_scalar, lnbf[:, :], yb[bs][:, :],
                               nsum[:, 0:1], rstd[:, 0:1], ALU.add, ALU.mult)
                    yw[bs] = (tokl, toksq)
                    transpose_to(ln2T3[:, :, qsub * 128:(qsub + 1) * 128], DT,
                                 lambda dd: lnbf[:, dd * 128:(dd + 1) * 128],
                                 ptrb, ptrb_tok, tokl)
                barrier()

            with ExitStack() as stF2:
                sbF2 = lambda n, sh, d: stF2.enter_context(nc.sbuf_tensor(n, sh, d))
                pkb = [sbF2(f"pkb{i}", [128, DR * 64], I32) for i in range(4)]
                sclb = [sbF2(f"sclb{i}", [128, 2 * DR], F32) for i in range(4)]
                m8b = [sbF2(f"m8b{i}", [128, 2 * DR], F32) for i in range(4)]
                tmpd = [sbF2(f"tmpd{i}", [128, DR * 128], I32) for i in range(2)]
                wdq = [sbF2(f"wdq{i}", [128, D], BF16) for i in range(4)]
                wT = [sbF2(f"wTf{i}", [128, DT * 128], BF16) for i in range(4)]
                wT3 = [w.ap().rearrange("p (d o) -> p d o", d=DT) for w in wT]
                sab = [sbF2(f"sab{i}", [128, Q], BF16) for i in range(2)]
                hab = [sbF2(f"hab{i}", [128, Q], BF16) for i in range(2)]
                ps_ab = [psF(f"psab{i}", [128, Q], F32) for i in range(2)]

                pk1_r = pk1.rearrange("(h r) t -> h (r t)", r=DR)
                pk3_r = pk3.rearrange("(h r) t -> h (r t)", r=DR)
                s1_r = s1.rearrange("(h g) -> h g", g=2 * DR)
                s3_r = s3.rearrange("(h g) -> h g", g=2 * DR)
                qsp = _splits(Q, 512)

                dPK = [dch[21], dch[22]]
                deq_rd = [0, 0, 0, 0]
                wT_mm = [0, 0, 0, 0]
                silu_last = 0
                tokh_slot = [0, 0]
                tokh_last = 0
                dHA = [dch[23], dch[3]]

                def dequant(buf_i, tsl, pk_r, s_r, row_t, col0, nh, FSCL):
                    ch = dPK[buf_i % 2]
                    ct.wait(sy, deq_rd[buf_i])
                    sy.dma_start(pkb[buf_i][:, 0:nh * 64],
                                 pk_r[row_t * 128:(row_t + 1) * 128,
                                      col0 * 64:(col0 + nh) * 64]) \
                        .then_inc(ch.s, 16)
                    sy.dma_start(sclb[buf_i][:, 0:2 * nh],
                                 s_r[row_t * 128:(row_t + 1) * 128,
                                     2 * col0:2 * (col0 + nh)]) \
                        .then_inc(ch.s, 16)
                    ch.n += 32
                    ch.wait(v)
                    vop(v.tensor_scalar_mul, m8b[buf_i][:, 0:2 * nh],
                        sclb[buf_i][:, 0:2 * nh], -8.0)
                    pk3v = pkb[buf_i].ap() \
                        .rearrange("p (r t) -> p r t", t=64)[:, 0:nh, :]
                    td = tmpd[tsl].ap() \
                        .rearrange("p (r h t) -> p r h t", h=2, t=64)[:, 0:nh]
                    vop(v.tensor_single_scalar, td[:, :, 0, :], pk3v, 4,
                        ALU.arith_shift_right)
                    vop(v.tensor_scalar, td[:, :, 1, :], pk3v, 15, 8,
                        ALU.bitwise_and, ALU.bitwise_xor)
                    wv = wdq[buf_i].ap() \
                        .rearrange("p (r h t) -> p r h t", h=2, t=64)[:, 0:nh]
                    sc_m = bass.AP(sclb[buf_i].ap().tensor, 0,
                                   [[FSCL, 128], [2, nh], [0, 64]])
                    sc_l = bass.AP(sclb[buf_i].ap().tensor, 1,
                                   [[FSCL, 128], [2, nh], [0, 64]])
                    m8_l = bass.AP(m8b[buf_i].ap().tensor, 1,
                                   [[FSCL, 128], [2, nh], [0, 64]])
                    vop(v.tensor_tensor, wv[:, :, 0, :], td[:, :, 0, :], sc_m,
                        ALU.mult)
                    vop(v.tensor_tensor, wv[:, :, 1, :], td[:, :, 1, :], sc_l,
                        ALU.mult)
                    tokd = vop(v.scalar_tensor_tensor, wv[:, :, 1, :],
                               wv[:, :, 1, :], 1.0, m8_l, ALU.mult, ALU.add)
                    return tokd

                for ht in range(NHT):
                    for wi, (pk_r, s_r) in enumerate([(pk1_r, s1_r),
                                                      (pk3_r, s3_r)]):
                        bi = 2 * (ht % 2) + wi
                        tokd = dequant(bi, wi, pk_r, s_r, ht, 0, DR, 2 * DR)
                        ct.wait(t, wT_mm[bi])
                        transpose_to(wT3[bi], DT,
                                     lambda dd: wdq[bi][:, dd * 128:(dd + 1) * 128],
                                     ptrb, ptrb_tok, tokd)
                        deq_rd[bi] = ct.n
                    for wi in range(2):
                        bi = 2 * (ht % 2) + wi
                        cv.wait(t, ptrb_tok[0])
                        cv.wait(t, ptrb_tok[1])
                        if wi == 0:
                            cs.wait(t, silu_last)
                        else:
                            cv.wait(t, tokh_last)
                        for no, nl in qsp:
                            for dd in range(DT):
                                ins = t.matmul(ps_ab[wi][:, no:no + nl],
                                               wT3[bi][:, dd, :],
                                               ln2T3[:, dd, no:no + nl],
                                               start=(dd == 0),
                                               stop=(dd == DT - 1))
                        wT_mm[bi] = ct.inc(ins)
                    hsl = ht % 2
                    ct.wait(s, wT_mm[2 * hsl + 0])
                    dHA[hsl].wait(s)
                    cv.wait(s, tokh_slot[hsl])
                    silu_last = sop(s.activation, sab[hsl][:, :],
                                    ps_ab[0][:, :], AF.Silu)
                    cs.wait(v, silu_last)
                    ct.wait(v, wT_mm[2 * hsl + 1])
                    dHA[hsl].wait(v)
                    tokh = vop(v.tensor_tensor, hab[hsl][:, :], sab[hsl][:, :],
                               ps_ab[1][:, :], ALU.mult)
                    tokh_slot[hsl] = tokh
                    tokh_last = tokh
                    cv.wait(sy, tokh)
                    sy.dma_start(hact_sc[ht * 128:(ht + 1) * 128, :],
                                 hab[hsl][:, :]).then_inc(dHA[hsl].s, 16)
                    dHA[hsl].n += 16
                barrier()

        with ExitStack() as stG:
            sbG = lambda n, sh, d: stG.enter_context(nc.sbuf_tensor(n, sh, d))
            psG = lambda n, sh, d: stG.enter_context(nc.psum_tensor(n, sh, d))
            HH = HR // 2 if HR % 2 == 0 else HR
            nhalves = HR // HH
            w2T = sbG("w2T", [128, HR * 512], BF16)
            w2T3 = w2T.ap().rearrange("p (h d) -> p h d", h=HR)
            hactb = [sbG(f"hactb{i}", [128, HR * 128], BF16) for i in range(2)]
            hactb3 = [w.ap().rearrange("p (h q) -> p h q", h=HR) for w in hactb]
            ytl = [sbG(f"ytl{i}", [128, 512], F32) for i in range(2)]
            otl = [sbG(f"otl{i}", [128, 512], F32) for i in range(2)]
            pkb = [sbG(f"pkbG{i}", [128, HH * 64], I32) for i in range(2)]
            sclb = [sbG(f"sclbG{i}", [128, 2 * HH], F32) for i in range(2)]
            m8b = [sbG(f"m8bG{i}", [128, 2 * HH], F32) for i in range(2)]
            tmpd = [sbG("tmpdG", [128, HH * 128], I32)]
            wdq = [sbG(f"wdqG{i}", [128, HH * 128], BF16) for i in range(2)]
            ptrb = [psG(f"ptrbG{i}", [128, 8 * 128], BF16) for i in range(2)]
            ps_f = [psG(f"psf{i}", [128, 512], F32) for i in range(2)]

            pk2_r = pk2.rearrange("(d r) t -> d (r t)", r=HR)
            s2_r = s2.rearrange("(d g) -> d g", g=2 * HR)
            dPK = [dch[21], dch[22]]
            deq_rd = [0, 0]
            ptrb_tok = [0, 0]

            def dequant2(buf_i, row_t, col0, nh):
                ch = dPK[buf_i]
                ct.wait(sy, deq_rd[buf_i])
                sy.dma_start(pkb[buf_i][:, 0:nh * 64],
                             pk2_r[row_t * 128:(row_t + 1) * 128,
                                   col0 * 64:(col0 + nh) * 64]).then_inc(ch.s, 16)
                sy.dma_start(sclb[buf_i][:, 0:2 * nh],
                             s2_r[row_t * 128:(row_t + 1) * 128,
                                  2 * col0:2 * (col0 + nh)]).then_inc(ch.s, 16)
                ch.n += 32
                ch.wait(v)
                vop(v.tensor_scalar_mul, m8b[buf_i][:, 0:2 * nh],
                    sclb[buf_i][:, 0:2 * nh], -8.0)
                pk3v = pkb[buf_i].ap().rearrange("p (r t) -> p r t", t=64)[:, 0:nh]
                td = tmpd[0].ap().rearrange("p (r h t) -> p r h t",
                                            h=2, t=64)[:, 0:nh]
                vop(v.tensor_single_scalar, td[:, :, 0, :], pk3v, 4,
                    ALU.arith_shift_right)
                vop(v.tensor_scalar, td[:, :, 1, :], pk3v, 15, 8,
                    ALU.bitwise_and, ALU.bitwise_xor)
                wv = wdq[buf_i].ap().rearrange("p (r h t) -> p r h t",
                                               h=2, t=64)[:, 0:nh]
                FS = 2 * HH
                sc_m = bass.AP(sclb[buf_i].ap().tensor, 0,
                               [[FS, 128], [2, nh], [0, 64]])
                sc_l = bass.AP(sclb[buf_i].ap().tensor, 1,
                               [[FS, 128], [2, nh], [0, 64]])
                m8_l = bass.AP(m8b[buf_i].ap().tensor, 1,
                               [[FS, 128], [2, nh], [0, 64]])
                vop(v.tensor_tensor, wv[:, :, 0, :], td[:, :, 0, :], sc_m,
                    ALU.mult)
                vop(v.tensor_tensor, wv[:, :, 1, :], td[:, :, 1, :], sc_l,
                    ALU.mult)
                tokd = vop(v.scalar_tensor_tensor, wv[:, :, 1, :],
                           wv[:, :, 1, :], 1.0, m8_l, ALU.mult, ALU.add)
                return tokd

            dH2 = [dch[12], dch[13]]
            dY2 = [dch[14], dch[15]]
            dO2 = [dch[16], dch[17]]
            dsp = _splits(D, 512)
            for di, (do, dl) in enumerate(dsp):
                ndd = dl // 128
                for i in range(ndd):
                    ddg = di * (512 // 128) + i
                    for half in range(nhalves):
                        bi = (i * nhalves + half) % 2
                        tokd = dequant2(bi, ddg, half * HH, HH)
                        transpose_to(
                            w2T3[:, half * HH:(half + 1) * HH,
                                 i * 128:(i + 1) * 128],
                            HH,
                            lambda hh: wdq[bi][:, hh * 128:(hh + 1) * 128],
                            ptrb, ptrb_tok, tokd)
                        deq_rd[bi] = ct.n
                w2_ready = cv.n
                h_mm = [0, 0]
                o_st = [0, 0]
                y_rd = [0, 0]
                for qsub in range(QT):
                    hsl = qsub % 2
                    ct.wait(sy, h_mm[hsl])
                    sy.dma_start(hactb3[hsl],
                                 hact_sc[:, qsub * 128:(qsub + 1) * 128]
                                 .rearrange("(h p) q -> p h q", p=128)) \
                        .then_inc(dH2[hsl].s, 16)
                    dH2[hsl].n += 16
                    cv.wait(sy, y_rd[hsl])
                    sy.dma_start(ytl[hsl][:, :],
                                 y_sc[qsub * 128:(qsub + 1) * 128, do:do + dl]) \
                        .then_inc(dY2[hsl].s, 16)
                    dY2[hsl].n += 16
                    dH2[hsl].wait(t)
                    t.wait_ge(cv.s, w2_ready)
                    cv.wait(t, y_rd[hsl])
                    for hh in range(HR):
                        ins = t.matmul(ps_f[hsl][:, 0:dl], hactb3[hsl][:, hh, :],
                                       w2T3[:, hh, 0:dl],
                                       start=(hh == 0), stop=(hh == HR - 1))
                    tokM = ct.inc(ins)
                    h_mm[hsl] = tokM
                    ct.wait(v, tokM)
                    dY2[hsl].wait(v)
                    dO2[hsl].wait(v)
                    tokO = vop(v.tensor_tensor, otl[hsl][:, 0:dl],
                               ps_f[hsl][:, 0:dl], ytl[hsl][:, 0:dl], ALU.add)
                    y_rd[hsl] = tokO
                    cv.wait(sy, tokO)
                    sy.dma_start(out_d[qsub * 128:(qsub + 1) * 128, do:do + dl],
                                 otl[hsl][:, 0:dl]).then_inc(dO2[hsl].s, 16)
                    dO2[hsl].n += 16
                if di + 1 < len(dsp):
                    barrier()
            dO2[0].wait(sy)
            dO2[1].wait(sy)
    return nc


def _rope_tables(n, qoff=0):
    hd = 128
    inv = 1.0 / (THETA ** (np.arange(0, hd, 2, dtype=np.float64) / hd))
    pos = np.arange(qoff, qoff + n, dtype=np.float64)
    fr = np.outer(pos, inv)
    emb = np.concatenate([fr, fr], axis=1)
    cos = np.cos(emb).T.astype(np.float32)
    sinm = np.sin(emb).T.astype(np.float32)
    sinm[:64] *= -1.0
    return np.ascontiguousarray(cos), np.ascontiguousarray(sinm)


def make_core_inputs(c: Cfg, core, x, wdict):
    b = core // c.cpb
    j = core % c.cpb
    qoff = j * c.Q
    cos, sinm = _rope_tables(c.S, 0)
    cosq, sinmq = _rope_tables(c.Q, qoff)
    mask = (np.arange(c.S)[:, None] <= (qoff + np.arange(c.Q))[None, :])
    d = {
        "x_full": np.ascontiguousarray(x[b]),
        "x_q": np.ascontiguousarray(x[b, qoff:qoff + c.Q]),
        "cosk": cos, "sinmk": sinm, "cosq": cosq, "sinmq": sinmq,
        "maskT": mask.astype(ml_dtypes.bfloat16),
        "identb": np.eye(128, dtype=ml_dtypes.bfloat16),
        "onesf": np.ones((1, 128), dtype=np.float32),
        "onesb": np.ones((128, 1), dtype=ml_dtypes.bfloat16),
        "constv": np.tile(np.array([[EPS, -0.5, c.SCALE, -CEXP]], np.float32),
                          (128, 1)),
    }
    d.update(wdict)
    return d


def make_wdict(L0_attn_w_q, L0_attn_w_k, L0_attn_w_v, L0_attn_w_o, L0_attn_w_lq,
               L0_attn_w_lk, L0_attn_w_lv, L0_attn_w_lo, L0_ffn_w1, L0_ffn_s1,
               L0_ffn_w2, L0_ffn_s2, L0_ffn_w3, L0_ffn_s3):
    return {
        "w_q": np.ascontiguousarray(np.asarray(L0_attn_w_q, np.float32)),
        "w_k": np.ascontiguousarray(np.asarray(L0_attn_w_k, np.float32)),
        "w_v": np.ascontiguousarray(np.asarray(L0_attn_w_v, np.float32)),
        "w_o": np.ascontiguousarray(np.asarray(L0_attn_w_o, np.float32)),
        "w_lq": np.ascontiguousarray(np.asarray(L0_attn_w_lq, np.float32)),
        "w_lk": np.ascontiguousarray(np.asarray(L0_attn_w_lk, np.float32)),
        "w_lv": np.ascontiguousarray(np.asarray(L0_attn_w_lv, np.float32)),
        "w_lo": np.ascontiguousarray(np.asarray(L0_attn_w_lo, np.float32)),
        "pk1": np.ascontiguousarray(np.asarray(L0_ffn_w1, np.int32)),
        "s1": np.ascontiguousarray(np.asarray(L0_ffn_s1, np.float32)),
        "pk2": np.ascontiguousarray(np.asarray(L0_ffn_w2, np.int32)),
        "s2": np.ascontiguousarray(np.asarray(L0_ffn_s2, np.float32)),
        "pk3": np.ascontiguousarray(np.asarray(L0_ffn_w3, np.int32)),
        "s3": np.ascontiguousarray(np.asarray(L0_ffn_s3, np.float32)),
    }


def kernel(L0_attn_hidden_states, L0_attn_w_q, L0_attn_w_k, L0_attn_w_v,
           L0_attn_w_o, L0_attn_w_lq, L0_attn_w_lk, L0_attn_w_lv, L0_attn_w_lo,
           L0_ffn_w1, L0_ffn_s1, L0_ffn_w2, L0_ffn_s2, L0_ffn_w3, L0_ffn_s3):
    c = Cfg()
    x = np.asarray(L0_attn_hidden_states, dtype=np.float32)
    wdict = make_wdict(L0_attn_w_q, L0_attn_w_k, L0_attn_w_v, L0_attn_w_o,
                       L0_attn_w_lq, L0_attn_w_lk, L0_attn_w_lv, L0_attn_w_lo,
                       L0_ffn_w1, L0_ffn_s1, L0_ffn_w2, L0_ffn_s2, L0_ffn_w3,
                       L0_ffn_s3)
    nc = build_program(c)
    in_maps = [make_core_inputs(c, core, x, wdict) for core in range(c.ncores)]
    res = run_bass_kernel_spmd(nc, in_maps, core_ids=list(range(c.ncores)))
    out = np.empty((c.B, c.S, c.D), dtype=np.float32)
    for core in range(c.ncores):
        b, j = core // c.cpb, core % c.cpb
        out[b, j * c.Q:(j + 1) * c.Q] = res.results[core]["out"]
    return out


# version 03
# speedup vs baseline: 1.1489x; 1.1489x over previous
import math
from contextlib import ExitStack

import numpy as np
import ml_dtypes

import concourse.bass as bass
import concourse.mybir as mybir
from concourse.bass_utils import run_bass_kernel_spmd

F32 = mybir.dt.float32
BF16 = mybir.dt.bfloat16
I32 = mybir.dt.int32
AF = mybir.ActivationFunctionType
ALU = mybir.AluOpType
AX = mybir.AxisListType

EPS = 1e-5
THETA = 10000.0
CEXP = 24.0


class Cfg:
    def __init__(self, D=4096, S=4096, Q=1024, H=32, KVH=8, LAT=1024, NHID=11008,
                 ncores=8, B=2):
        self.D, self.S, self.Q, self.H, self.KVH, self.LAT, self.NHID = \
            D, S, Q, H, KVH, LAT, NHID
        self.ncores, self.B = ncores, B
        self.HD = 128
        self.DT = D // 128
        self.LT = LAT // 128
        self.KT = S // 128
        self.QT = Q // 128
        self.NHT = NHID // 128
        self.DR = D // 128
        self.HR = NHID // 128
        self.RC = min(1024, S)
        self.NRC = S // self.RC
        self.G = H // KVH
        self.SCALE = 1.0 / math.sqrt(self.HD)
        self.cpb = ncores // B


def _splits(total, step):
    return [(o, min(step, total - o)) for o in range(0, total, step)]


class Ch:

    def __init__(self, nc, stack, name):
        self.s = stack.enter_context(nc.semaphore(name))
        self.n = 0

    def inc(self, instr, k=1):
        instr.then_inc(self.s, k)
        self.n += k
        return self.n

    def wait(self, eng, n=None):
        eng.wait_ge(self.s, self.n if n is None else n)


def build_program(c: Cfg):
    nc = bass.Bass("TRN2", target_bir_lowering=False)
    D, S, Q, H, KVH, LAT, NHID = c.D, c.S, c.Q, c.H, c.KVH, c.LAT, c.NHID
    DT, LT, KT, QT, NHT, DR, HR = c.DT, c.LT, c.KT, c.QT, c.NHT, c.DR, c.HR
    RC, NRC, G = c.RC, c.NRC, c.G

    dtt = nc.dram_tensor
    xf = dtt("x_full", [S, D], F32, kind="ExternalInput").ap()
    xq = dtt("x_q", [Q, D], F32, kind="ExternalInput").ap()
    w_q = dtt("w_q", [D, H * 128], F32, kind="ExternalInput").ap()
    w_k = dtt("w_k", [D, KVH * 128], F32, kind="ExternalInput").ap()
    w_v = dtt("w_v", [D, KVH * 128], F32, kind="ExternalInput").ap()
    w_o = dtt("w_o", [H * 128, D], F32, kind="ExternalInput").ap()
    w_lq = dtt("w_lq", [D, LAT], F32, kind="ExternalInput").ap()
    w_lk = dtt("w_lk", [LAT, H * 128], F32, kind="ExternalInput").ap()
    w_lv = dtt("w_lv", [D, H * 128], F32, kind="ExternalInput").ap()
    w_lo = dtt("w_lo", [H * 128, D], F32, kind="ExternalInput").ap()
    pk1 = dtt("pk1", [NHID * DR, 64], I32, kind="ExternalInput").ap()
    s1 = dtt("s1", [NHID * DR * 2], F32, kind="ExternalInput").ap()
    pk2 = dtt("pk2", [D * HR, 64], I32, kind="ExternalInput").ap()
    s2 = dtt("s2", [D * HR * 2], F32, kind="ExternalInput").ap()
    pk3 = dtt("pk3", [NHID * DR, 64], I32, kind="ExternalInput").ap()
    s3 = dtt("s3", [NHID * DR * 2], F32, kind="ExternalInput").ap()
    cosk = dtt("cosk", [128, S], F32, kind="ExternalInput").ap()
    sinmk = dtt("sinmk", [128, S], F32, kind="ExternalInput").ap()
    cosq = dtt("cosq", [128, Q], F32, kind="ExternalInput").ap()
    sinmq = dtt("sinmq", [128, Q], F32, kind="ExternalInput").ap()
    maskT = dtt("maskT", [S, Q], BF16, kind="ExternalInput").ap()
    identb = dtt("identb", [128, 128], BF16, kind="ExternalInput").ap()
    onesf = dtt("onesf", [1, 128], F32, kind="ExternalInput").ap()
    onesb = dtt("onesb", [128, 1], BF16, kind="ExternalInput").ap()
    constv = dtt("constv", [128, 4], F32, kind="ExternalInput").ap()
    out_d = dtt("out", [Q, D], F32, kind="ExternalOutput").ap()

    k_sc = dtt("k_sc", [KVH * 128, S], BF16, kind="Internal").ap()
    v_sc = dtt("v_sc", [128, KVH * S], BF16, kind="Internal").ap()
    lk_sc = dtt("lk_sc", [H * 128, S], BF16, kind="Internal").ap()
    lv_sc = dtt("lv_sc", [128, H * S], BF16, kind="Internal").ap()
    q_sc = dtt("q_sc", [H * 128, Q], BF16, kind="Internal").ap()
    ctx_sc = dtt("ctx_sc", [2 * H * 128, Q], BF16, kind="Internal").ap()
    x1a_sc = dtt("x1a_sc", [Q, D], F32, kind="Internal").ap()
    y_sc = dtt("y_sc", [Q, D], F32, kind="Internal").ap()
    hact_sc = dtt("hact_sc", [NHT * 128, Q], BF16, kind="Internal").ap()

    sy, g, t, v, s = nc.sync, nc.gpsimd, nc.tensor, nc.vector, nc.scalar

    with ExitStack() as st0:
        idt = st0.enter_context(nc.sbuf_tensor("idt", [128, 128], BF16))
        cb = st0.enter_context(nc.sbuf_tensor("cb", [128, 4], F32))
        onef = st0.enter_context(nc.sbuf_tensor("onef", [1, 128], F32))
        oneb = st0.enter_context(nc.sbuf_tensor("oneb", [128, 1], BF16))

        cv = Ch(nc, st0, "cv")
        cs = Ch(nc, st0, "cs")
        ct = Ch(nc, st0, "ct")
        dch = [Ch(nc, st0, f"d{i}") for i in range(24)]
        gch = [Ch(nc, st0, f"g{i}") for i in range(4)]

        blk = st0.enter_context(nc.Block())

        def vop(fn, *a, **kw):
            v.wait_ge(cv.s, cv.n)
            return cv.inc(fn(*a, **kw))

        def sop(fn, *a, **kw):
            s.wait_ge(cs.s, cs.n)
            return cs.inc(fn(*a, **kw))

        def barrier():
            for ch_ in dch:
                if ch_.n:
                    ch_.wait(sy)
            for ch_ in gch:
                if ch_.n:
                    ch_.wait(g)
            nc.all_engine_barrier()

        ch0 = dch[0]
        sy.dma_start(idt[:, :], identb).then_inc(ch0.s, 16); ch0.n += 16
        sy.dma_start(onef[:, :], onesf).then_inc(ch0.s, 16); ch0.n += 16
        sy.dma_start(oneb[:, :], onesb).then_inc(ch0.s, 16); ch0.n += 16
        sy.dma_start(cb[:, :], constv).then_inc(ch0.s, 16); ch0.n += 16
        ch0.wait(t)
        ch0.wait(v)
        ch0.wait(s)

        def transpose_to(dst3, n_tiles, src_ap_fn, ptrb, ptrb_tok, src_cv_tok):
            if src_cv_tok:
                cv.wait(t, src_cv_tok)
            last = 0
            for gi, (g0, gl) in enumerate(_splits(n_tiles, 8)):
                bsel = gi % 2
                cv.wait(t, ptrb_tok[bsel])
                for j in range(gl):
                    ins = t.transpose(ptrb[bsel][:, j * 128:(j + 1) * 128],
                                      src_ap_fn(g0 + j), idt[:, :])
                tokt = ct.inc(ins)
                ct.wait(v, tokt)
                srcv = ptrb[bsel].ap().rearrange("p (j k) -> p j k", k=128)[:, 0:gl, :]
                ptrb_tok[bsel] = vop(v.tensor_copy, dst3[:, g0:g0 + gl, :], srcv)
                last = ptrb_tok[bsel]
            return last

        def ln_rows(src_ap, xt, sq, lnbf, stats, ld_ch, war_cv, war_cs, extra=None):
            cv.wait(sy, war_cv)
            cs.wait(sy, war_cs)
            sy.dma_start(xt[:, :], src_ap).then_inc(ld_ch.s, 16)
            ld_ch.n += 16
            if extra is not None:
                extra()
            nsum, ssq, m2, varv, rstd = stats
            ld_ch.wait(s)
            s.wait_ge(cv.s, cv.n)
            toksq = sop(s.activation, sq[:, :], xt[:, :], AF.Square,
                        accum_out=ssq[:, :])
            ld_ch.wait(v)
            vop(v.tensor_reduce, nsum[:, :], xt[:, :], AX.X, ALU.add)
            vop(v.tensor_scalar_mul, nsum[:, :], nsum[:, :], -1.0 / D)
            vop(v.tensor_tensor, m2[:, :], nsum[:, :], nsum[:, :], ALU.mult)
            cs.wait(v, toksq)
            vop(v.tensor_scalar_mul, varv[:, :], ssq[:, :], 1.0 / D)
            tokv = vop(v.tensor_tensor, varv[:, :], varv[:, :], m2[:, :],
                       ALU.subtract)
            cv.wait(s, tokv)
            sop(s.activation, varv[:, :], varv[:, :], AF.Ln, bias=cb[:, 0:1])
            toks = sop(s.activation, rstd[:, :], varv[:, :], AF.Exp, scale=cb[:, 1:2])
            cs.wait(v, toks)
            tokl = vop(v.tensor_scalar, lnbf[:, :], xt[:, :],
                       nsum[:, 0:1], rstd[:, 0:1], ALU.add, ALU.mult)
            return tokl, toksq

        with ExitStack() as stA:
            sbA = lambda n, sh, d: stA.enter_context(nc.sbuf_tensor(n, sh, d))
            psA = lambda n, sh, d: stA.enter_context(nc.psum_tensor(n, sh, d))
            lnT = sbA("lnT", [128, DT * RC], BF16)
            lnT3 = lnT.ap().rearrange("p (d r) -> p d r", d=DT)
            lqT = sbA("lqT", [128, LT * RC], BF16)
            lqT3 = lqT.ap().rearrange("p (l r) -> p l r", l=LT)
            xt = [sbA(f"xt{i}", [128, D], F32) for i in range(2)]
            sqj = sbA("sqj", [128, D], F32)
            lnbf = sbA("lnbf", [128, D], BF16)
            stats = [sbA(f"st{i}", [128, 1], F32) for i in range(5)]
            wt_bf = [sbA(f"wtbf{i}", [128, DT * 128], BF16) for i in range(2)]
            wt3 = [w.ap().rearrange("p (d o) -> p d o", d=DT) for w in wt_bf]
            wlk_bf = [sbA(f"wlkbf{i}", [128, LT * 128], BF16) for i in range(2)]
            wlk3 = [w.ap().rearrange("p (l o) -> p l o", l=LT) for w in wlk_bf]
            cosc = sbA("cosc", [128, RC], F32)
            sinc = sbA("sinc", [128, RC], F32)
            rt1 = sbA("rt1", [128, RC], F32)
            rt2 = sbA("rt2", [128, RC], F32)
            obf = [sbA(f"obf{i}", [128, RC], BF16) for i in range(3)]
            vtb = [sbA(f"vtb{i}", [128, RC], BF16) for i in range(2)]
            ptrb = [psA(f"ptrb{i}", [128, 8 * 128], BF16) for i in range(2)]
            pso = [psA(f"pso{i}", [128, min(512, RC)], F32) for i in range(2)]

            ch1 = dch[1]

            xf3 = xf.rearrange("(a p) d -> a p d", p=128)
            xq3 = xq.rearrange("(a p) d -> a p d", p=128)
            wq_r = w_q.rearrange("(a b) c -> b a c", b=128)
            wk_r = w_k.rearrange("(a b) c -> b a c", b=128)
            wv_r = w_v.rearrange("(a b) c -> b a c", b=128)
            wlq_r = w_lq.rearrange("(a b) c -> b a c", b=128)
            wlv_r = w_lv.rearrange("(a b) c -> b a c", b=128)
            wlk_r = w_lk.rearrange("(a b) c -> b a c", b=128)

            ptrb_tok = [0, 0]

            def proj_chunk(is_q, rc_idx, nrows):
                r0 = rc_idx * RC
                nrt = nrows // 128
                if is_q:
                    sy.dma_start(cosc[:, 0:Q], cosq).then_inc(ch1.s, 16)
                    sy.dma_start(sinc[:, 0:Q], sinmq).then_inc(ch1.s, 16)
                else:
                    sy.dma_start(cosc[:, :], cosk[:, r0:r0 + RC]) \
                        .then_inc(ch1.s, 16)
                    sy.dma_start(sinc[:, :], sinmk[:, r0:r0 + RC]) \
                        .then_inc(ch1.s, 16)
                ch1.n += 32
                ch1.wait(v)
                xt_war = [(0, 0), (0, 0)]
                for ti in range(nrt):
                    bs = ti % 2
                    src = (xq3 if is_q else xf3)[(r0 // 128) + ti, :, :]
                    tokl, toksq = ln_rows(src, xt[bs], sqj, lnbf, stats,
                                          dch[2 + bs], *xt_war[bs])
                    xt_war[bs] = (tokl, toksq)
                    transpose_to(lnT3[:, :, ti * 128:(ti + 1) * 128], DT,
                                 lambda dd: lnbf[:, dd * 128:(dd + 1) * 128],
                                 ptrb, ptrb_tok, tokl)
                lnT_ready = cv.n

                if is_q:
                    wlist = [("q", wq_r, hh) for hh in range(H)]
                else:
                    wlist = ([("k", wk_r, hh) for hh in range(KVH)]
                             + [("v", wv_r, hh) for hh in range(KVH)]
                             + [("lq", wlq_r, oc) for oc in range(LT)]
                             + [("lv", wlv_r, hh) for hh in range(H)])
                cos_t, sin_t = cosc, sinc
                cpos0 = 0
                nfs = _splits(nrows, 512)

                wt_last_mm = [0, 0]
                pso_tok = [0, 0]
                obf_rd = [None, None, None]
                vtb_rd = [None, None]

                def wait_reader(eng, rd):
                    if rd is None:
                        return
                    kind, a, b = rd
                    if kind == "cv":
                        cv.wait(eng, a)
                    elif kind == "ct":
                        ct.wait(eng, a)
                    else:
                        a.wait(eng, b)

                def emit_proj(wi, ty, wr, oc, lq_ready):
                    nonlocal ptrb_tok
                    ws = wi % 2
                    gc = gch[ws]
                    ct.wait(g, wt_last_mm[ws])
                    if ty == "lk":
                        g.dma_start(wlk_bf[ws][:, :],
                                    wr[:, :, oc * 128:(oc + 1) * 128]) \
                            .then_inc(gc.s, 16)
                        lhs3, nlhs, rhs3, rhs_rdy = wlk3[ws], LT, lqT3, lq_ready
                    else:
                        g.dma_start(wt_bf[ws][:, :],
                                    wr[:, :, oc * 128:(oc + 1) * 128]) \
                            .then_inc(gc.s, 16)
                        lhs3, nlhs, rhs3, rhs_rdy = wt3[ws], DT, lnT3, lnT_ready
                    gc.n += 16
                    ob_i = wi % 3
                    ob = obf[ob_i]
                    rope_toks = []
                    for nf, (no, nl) in enumerate(nfs):
                        po = (wi * len(nfs) + nf) % 2
                        cv.wait(t, pso_tok[po])
                        gc.wait(t)
                        t.wait_ge(cv.s, rhs_rdy)
                        for dd in range(nlhs):
                            ins = t.matmul(pso[po][:, 0:nl], lhs3[:, dd, :],
                                           rhs3[:, dd, no:no + nl],
                                           start=(dd == 0), stop=(dd == nlhs - 1))
                        mm_tok = ct.inc(ins)
                        wt_last_mm[ws] = mm_tok
                        rows = slice(no, no + nl)
                        ct.wait(v, mm_tok)
                        if nf == 0:
                            wait_reader(v, obf_rd[ob_i])
                        if ty in ("k", "lk", "q"):
                            cp = cos_t[:, cpos0 + no:cpos0 + no + nl]
                            sp = sin_t[:, cpos0 + no:cpos0 + no + nl]
                            vop(v.tensor_tensor, rt1[:, rows], pso[po][:, 0:nl],
                                cp, ALU.mult)
                            vop(v.tensor_tensor, rt2[0:64, rows],
                                pso[po][64:128, 0:nl], sp[0:64, :], ALU.mult)
                            vop(v.tensor_tensor, rt2[64:128, rows],
                                pso[po][0:64, 0:nl], sp[64:128, :], ALU.mult)
                            tokr = vop(v.tensor_tensor, ob[:, rows],
                                       rt1[:, rows], rt2[:, rows], ALU.add)
                        else:
                            tokr = vop(v.tensor_copy, ob[:, rows],
                                       pso[po][:, 0:nl])
                        pso_tok[po] = tokr
                        rope_toks.append(tokr)
                    if ty in ("k", "lk", "q"):
                        c_st = dch[6 + ob_i]
                        cv.wait(sy, rope_toks[-1])
                        dst = {"k": k_sc, "lk": lk_sc, "q": q_sc}[ty]
                        if ty == "q":
                            tgt = dst[oc * 128:(oc + 1) * 128, 0:nrows]
                        else:
                            tgt = dst[oc * 128:(oc + 1) * 128, r0:r0 + nrows]
                        sy.dma_start(tgt, ob[:, 0:nrows]).then_inc(c_st.s, 16)
                        c_st.n += 16
                        obf_rd[ob_i] = ("dma", c_st, c_st.n)
                    elif ty == "lq":
                        pass
                    if ty in ("v", "lv"):
                        vs = wi % 2
                        wait_reader(v, vtb_rd[vs])
                        vt3 = vtb[vs].ap().rearrange("p (j k) -> p j k", k=128)
                        tokT = transpose_to(vt3, nrt,
                                            lambda j: ob[:, j * 128:(j + 1) * 128],
                                            ptrb, ptrb_tok, rope_toks[-1])
                        obf_rd[ob_i] = ("ct", ct.n, None)
                        c_st = dch[9 + vs]
                        cv.wait(sy, tokT)
                        dst = v_sc if ty == "v" else lv_sc
                        dst_r = dst[:, oc * S + r0:oc * S + r0 + nrows]
                        sy.dma_start(dst_r, vtb[vs][:, 0:nrows]).then_inc(c_st.s, 16)
                        c_st.n += 16
                        vtb_rd[vs] = ("dma", c_st, c_st.n)

                def emit_lq(wi, oc):
                    ws = wi % 2
                    gc = gch[ws]
                    ct.wait(g, wt_last_mm[ws])
                    g.dma_start(wt_bf[ws][:, :],
                                wlq_r[:, :, oc * 128:(oc + 1) * 128]) \
                        .then_inc(gc.s, 16)
                    gc.n += 16
                    for nf, (no, nl) in enumerate(nfs):
                        po = (wi * len(nfs) + nf) % 2
                        cv.wait(t, pso_tok[po])
                        gc.wait(t)
                        t.wait_ge(cv.s, lnT_ready)
                        for dd in range(DT):
                            ins = t.matmul(pso[po][:, 0:nl], wt3[ws][:, dd, :],
                                           lnT3[:, dd, no:no + nl],
                                           start=(dd == 0), stop=(dd == DT - 1))
                        mm_tok = ct.inc(ins)
                        wt_last_mm[ws] = mm_tok
                        ct.wait(v, mm_tok)
                        pso_tok[po] = vop(v.tensor_copy,
                                          lqT3[:, oc, no:no + nl],
                                          pso[po][:, 0:nl])

                wi = 0
                for ty, wr, oc in wlist:
                    if ty == "lq":
                        emit_lq(wi, oc)
                    else:
                        emit_proj(wi, ty, wr, oc, 0)
                    wi += 1
                if not is_q:
                    lq_ready = cv.n
                    for oc in range(H):
                        emit_proj(wi, "lk", wlk_r, oc, lq_ready)
                        wi += 1
                barrier()

            for rc_idx in range(NRC):
                proj_chunk(False, rc_idx, RC)
            proj_chunk(True, 0, Q)
            barrier()

        with ExitStack() as stB:
            sbB = lambda n, sh, d: stB.enter_context(nc.sbuf_tensor(n, sh, d))
            psB = lambda n, sh, d: stB.enter_context(nc.psum_tensor(n, sh, d))
            maskb = sbB("maskb", [128, KT * Q], BF16)
            mask3 = maskb.ap().rearrange("p (k q) -> p k q", k=KT)
            kbuf = [sbB(f"kb{i}", [128, S], BF16) for i in range(2)]
            vbuf = [sbB(f"vb{i}", [128, KT * 128], BF16) for i in range(2)]
            lkbuf = [sbB(f"lkb{i}", [128, S], BF16) for i in range(2)]
            lvbuf = [sbB(f"lvb{i}", [128, KT * 128], BF16) for i in range(2)]
            qbuf = [sbB(f"qb{i}", [128, Q], BF16) for i in range(2)]
            pmb = [sbB(f"pm{i}", [128, Q], BF16) for i in range(3)]
            ctxb = [sbB(f"ctx{i}", [128, Q], BF16) for i in range(2)]
            recipb = sbB("recipb", [1, Q], F32)
            rbb = sbB("rbb", [128, Q], F32)
            ps_s = [psB(f"pss{i}", [128, Q], F32) for i in range(2)]
            ps_ctx = psB("psctx", [128, Q], F32)
            ps_sum = psB("pssum", [1, Q], F32)

            chm = dch[11]
            sy.dma_start(mask3, maskT.rearrange("(k p) q -> p k q", p=128)) \
                .then_inc(chm.s, 16)
            chm.n += 16
            chm.wait(v)
            chm.wait(s)

            dKV = [dch[12], dch[13]]
            dLTc = [dch[14], dch[15]]
            dCX = [dch[16], dch[17]]
            kv_tok = {}
            done_tok = {}

            def load_kv(hg):
                slot = hg % 2
                ct.wait(sy, done_tok.get(("kv", hg - 2), 0))
                ch = dKV[slot]
                sy.dma_start(kbuf[slot][:, :],
                             k_sc[hg * 128:(hg + 1) * 128, :]).then_inc(ch.s, 16)
                sy.dma_start(vbuf[slot][:, :],
                             v_sc[:, hg * S:(hg + 1) * S]).then_inc(ch.s, 16)
                ch.n += 32
                kv_tok[("kv", hg)] = ch.n

            def load_lat(h):
                slot = h % 2
                ct.wait(sy, done_tok.get(("lt", h - 2), 0))
                ch = dLTc[slot]
                sy.dma_start(lkbuf[slot][:, :],
                             lk_sc[h * 128:(h + 1) * 128, :]).then_inc(ch.s, 16)
                sy.dma_start(lvbuf[slot][:, :],
                             lv_sc[:, h * S:(h + 1) * S]).then_inc(ch.s, 16)
                sy.dma_start(qbuf[slot][:, :],
                             q_sc[h * 128:(h + 1) * 128, :]).then_inc(ch.s, 16)
                ch.n += 48
                kv_tok[("lt", h)] = ch.n

            load_kv(0)
            load_lat(0)

            exp_tok = [0] * KT
            pm_tok = [0] * KT
            pv_tok = [0] * KT
            ctx_dma = [0, 0]
            tokRB_prev = 0
            qsp = _splits(Q, 512)

            for h in range(H):
                if h + 1 < H:
                    if (h + 1) % G == 0:
                        load_kv((h + 1) // G)
                    load_lat(h + 1)
                for a in range(2):
                    lslot = h % 2
                    if a == 0:
                        kvs = (h // G) % 2
                        kb_, vb_ = kbuf[kvs], vbuf[kvs]
                        dKV[kvs].wait(t, kv_tok[("kv", h // G)])
                    else:
                        kb_, vb_ = lkbuf[lslot], lvbuf[lslot]
                    dLTc[lslot].wait(t, kv_tok[("lt", h)])
                    qb_ = qbuf[lslot]
                    vb3 = vb_.ap().rearrange("p (k j) -> p k j", j=128)
                    for kt in range(KT + 2):
                        if kt < KT:
                            sl = kt % 2
                            pm = pmb[kt % 3]
                            cs.wait(t, exp_tok[(kt - 2) % KT])
                            if kt == 0:
                                cv.wait(t, tokRB_prev)
                            for qo, ql in qsp:
                                ins = t.matmul(ps_s[sl][:, qo:qo + ql],
                                               kb_[:, kt * 128:(kt + 1) * 128],
                                               qb_[:, qo:qo + ql],
                                               start=True, stop=True)
                            tokS = ct.inc(ins)
                            ct.wait(s, tokS)
                            ct.wait(s, pv_tok[(kt - 3) % KT])
                            s.wait_ge(cs.s, cs.n)
                            exp_tok[kt] = cs.inc(s.activation(
                                pm[:, :], ps_s[sl][:, :], AF.Exp,
                                scale=cb[:, 2:3], bias=cb[:, 3:4]))
                            cs.wait(v, exp_tok[kt])
                            pm_tok[kt] = vop(v.tensor_tensor, pm[:, :],
                                             pm[:, :], mask3[:, kt, :],
                                             ALU.mult)
                        if kt >= 2:
                            k2 = kt - 2
                            pm2 = pmb[k2 % 3]
                            cv.wait(t, pm_tok[k2])
                            for qo, ql in qsp:
                                t.matmul(ps_sum[0:1, qo:qo + ql], oneb[:, :],
                                         pm2[:, qo:qo + ql],
                                         start=(k2 == 0), stop=(k2 == KT - 1))
                                ins = t.matmul(ps_ctx[:, qo:qo + ql],
                                               vb3[:, k2, :],
                                               pm2[:, qo:qo + ql],
                                               start=(k2 == 0), stop=(k2 == KT - 1))
                            pv_tok[k2] = ct.inc(ins)
                    ct.wait(v, pv_tok[KT - 1])
                    tokR = vop(v.reciprocal, recipb[:, :], ps_sum[:, :])
                    cv.wait(t, tokR)
                    cs.wait(t, exp_tok[KT - 1])
                    for qo, ql in qsp:
                        ins = t.matmul(ps_s[KT % 2][:, qo:qo + ql], onef[:, :],
                                       recipb[0:1, qo:qo + ql],
                                       start=True, stop=True)
                    tokB = ct.inc(ins)
                    ct.wait(v, tokB)
                    tokRB = vop(v.tensor_copy, rbb[:, :], ps_s[KT % 2][:, :])
                    tokRB_prev = tokRB
                    cslot = a
                    dCX[cslot].wait(v, ctx_dma[cslot])
                    tokC = vop(v.tensor_tensor, ctxb[cslot][:, :], ps_ctx[:, :],
                               rbb[:, :], ALU.mult)
                    cv.wait(sy, tokC)
                    sy.dma_start(
                        ctx_sc[(a * H + h) * 128:(a * H + h + 1) * 128, :],
                        ctxb[cslot][:, :]).then_inc(dCX[cslot].s, 16)
                    dCX[cslot].n += 16
                    ctx_dma[cslot] = dCX[cslot].n
                    cv.wait(t, tokC)
                    if a == 1:
                        done_tok[("kv", h // G)] = ct.n
                        done_tok[("lt", h)] = ct.n
            barrier()

        with ExitStack() as stO:
            sbO = lambda n, sh, d: stO.enter_context(nc.sbuf_tensor(n, sh, d))
            psO = lambda n, sh, d: stO.enter_context(nc.psum_tensor(n, sh, d))
            ctxa = sbO("ctxa", [128, H * Q], BF16)
            ctxa3 = ctxa.ap().rearrange("p (h q) -> p h q", h=H)
            wob = [sbO(f"wob{i}", [128, H * 512], BF16) for i in range(2)]
            wob3 = [w.ap().rearrange("p (h d) -> p h d", h=H) for w in wob]
            o_bf = [sbO(f"o_bf{i}", [128, 512], F32) for i in range(2)]
            x1t = [sbO(f"x1t{i}", [128, 512], F32) for i in range(2)]
            ps_op = [psO(f"psop{i}", [128, 512], F32) for i in range(2)]

            dsp = _splits(D, 512)
            for p_i, (wsrc, dst_sc) in enumerate([(w_o, x1a_sc), (w_lo, y_sc)]):
                wsrc_r = wsrc.rearrange("(a b) c -> b a c", b=128)
                cca = dch[18]
                sy.dma_start(ctxa3,
                             ctx_sc[p_i * H * 128:(p_i + 1) * H * 128, :]
                             .rearrange("(h p) q -> p h q", p=128)) \
                    .then_inc(cca.s, 16)
                cca.n += 16
                cca.wait(t)
                wo_mm = [0, 0]
                ps_tok = [0, 0]
                dXc = [dch[12], dch[13]]
                dOc = [dch[19], dch[20]]
                x1_rd = [0, 0]
                for di, (do, dl) in enumerate(dsp):
                    wslot = di % 2
                    gc = gch[2 + wslot]
                    ct.wait(g, wo_mm[wslot])
                    g.dma_start(wob[wslot][:, 0:H * dl],
                                wsrc_r[:, :, do:do + dl]).then_inc(gc.s, 16)
                    gc.n += 16
                    for qsub in range(QT):
                        pslot = qsub % 2
                        bsel = (di * QT + qsub) % 2
                        if p_i == 1:
                            cv.wait(sy, x1_rd[bsel])
                            sy.dma_start(
                                x1t[bsel][:, 0:dl],
                                x1a_sc[qsub * 128:(qsub + 1) * 128, do:do + dl]) \
                                .then_inc(dXc[bsel].s, 16)
                            dXc[bsel].n += 16
                        cv.wait(t, ps_tok[pslot])
                        gc.wait(t)
                        for ht in range(H):
                            ins = t.matmul(
                                ps_op[pslot][:, 0:dl],
                                ctxa3[:, ht, qsub * 128:(qsub + 1) * 128],
                                wob3[wslot][:, ht, 0:dl],
                                start=(ht == 0), stop=(ht == H - 1))
                        tokM = ct.inc(ins)
                        wo_mm[wslot] = tokM
                        ct.wait(v, tokM)
                        dOc[bsel].wait(v)
                        if p_i == 0:
                            tokE = vop(v.tensor_copy, o_bf[bsel][:, 0:dl],
                                       ps_op[pslot][:, 0:dl])
                        else:
                            dXc[bsel].wait(v)
                            tokE = vop(v.tensor_tensor, o_bf[bsel][:, 0:dl],
                                       ps_op[pslot][:, 0:dl],
                                       x1t[bsel][:, 0:dl], ALU.add)
                            x1_rd[bsel] = tokE
                        ps_tok[pslot] = tokE
                        cv.wait(sy, tokE)
                        sy.dma_start(
                            dst_sc[qsub * 128:(qsub + 1) * 128, do:do + dl],
                            o_bf[bsel][:, 0:dl]).then_inc(dOc[bsel].s, 16)
                        dOc[bsel].n += 16
                barrier()

        with ExitStack() as stF:
            sbF = lambda n, sh, d: stF.enter_context(nc.sbuf_tensor(n, sh, d))
            psF = lambda n, sh, d: stF.enter_context(nc.psum_tensor(n, sh, d))
            ln2T = sbF("ln2T", [128, DT * Q], BF16)
            ln2T3 = ln2T.ap().rearrange("p (d r) -> p d r", d=DT)
            ptrb = [psF(f"ptrbF{i}", [128, 8 * 128], BF16) for i in range(2)]
            with ExitStack() as stF1:
                sbF1 = lambda n, sh, d: stF1.enter_context(nc.sbuf_tensor(n, sh, d))
                xt = [sbF1(f"xtF{i}", [128, D], F32) for i in range(2)]
                yb = [sbF1(f"ybF{i}", [128, D], F32) for i in range(2)]
                sqj = sbF1("sqjF", [128, D], F32)
                lnbf = sbF1("lnbfF", [128, D], BF16)
                stats = [sbF1(f"stF{i}", [128, 1], F32) for i in range(5)]
                ptrb_tok = [0, 0]
                yw = [(0, 0), (0, 0)]
                xq3b = xq.rearrange("(a p) d -> a p d", p=128)
                dYs = [dch[4], dch[5]]
                for qsub in range(QT):
                    bs = qsub % 2
                    ch = dch[2 + bs]

                    def extra():
                        dYs[bs].wait(sy)
                        sy.dma_start(yb[bs][:, :],
                                     y_sc[qsub * 128:(qsub + 1) * 128, :]) \
                            .then_inc(ch.s, 16)
                        ch.n += 16

                    tok_add = [0]
                    cv.wait(sy, yw[bs][0])
                    cs.wait(sy, yw[bs][1])
                    sy.dma_start(xt[bs][:, :], xq3b[qsub, :, :]) \
                        .then_inc(ch.s, 16)
                    ch.n += 16
                    extra()
                    ch.wait(v)
                    toky = vop(v.tensor_tensor, yb[bs][:, :], yb[bs][:, :],
                               xt[bs][:, :], ALU.add)
                    cv.wait(sy, toky)
                    sy.dma_start(y_sc[qsub * 128:(qsub + 1) * 128, :],
                                 yb[bs][:, :]).then_inc(dYs[bs].s, 16)
                    dYs[bs].n += 16
                    nsum, ssq, m2, varv, rstd = stats
                    cv.wait(s, toky)
                    s.wait_ge(cv.s, cv.n)
                    toksq = sop(s.activation, sqj[:, :], yb[bs][:, :], AF.Square,
                                accum_out=ssq[:, :])
                    vop(v.tensor_reduce, nsum[:, :], yb[bs][:, :], AX.X, ALU.add)
                    vop(v.tensor_scalar_mul, nsum[:, :], nsum[:, :], -1.0 / D)
                    vop(v.tensor_tensor, m2[:, :], nsum[:, :], nsum[:, :],
                        ALU.mult)
                    cs.wait(v, toksq)
                    vop(v.tensor_scalar_mul, varv[:, :], ssq[:, :], 1.0 / D)
                    tokv = vop(v.tensor_tensor, varv[:, :], varv[:, :], m2[:, :],
                               ALU.subtract)
                    cv.wait(s, tokv)
                    sop(s.activation, varv[:, :], varv[:, :], AF.Ln, bias=cb[:, 0:1])
                    toks = sop(s.activation, rstd[:, :], varv[:, :], AF.Exp,
                               scale=cb[:, 1:2])
                    cs.wait(v, toks)
                    tokl = vop(v.tensor_scalar, lnbf[:, :], yb[bs][:, :],
                               nsum[:, 0:1], rstd[:, 0:1], ALU.add, ALU.mult)
                    yw[bs] = (tokl, toksq)
                    transpose_to(ln2T3[:, :, qsub * 128:(qsub + 1) * 128], DT,
                                 lambda dd: lnbf[:, dd * 128:(dd + 1) * 128],
                                 ptrb, ptrb_tok, tokl)
                barrier()

            with ExitStack() as stF2:
                sbF2 = lambda n, sh, d: stF2.enter_context(nc.sbuf_tensor(n, sh, d))
                pkb = [sbF2(f"pkb{i}", [128, DR * 64], I32) for i in range(4)]
                sclb = [sbF2(f"sclb{i}", [128, 2 * DR], F32) for i in range(4)]
                m8b = [sbF2(f"m8b{i}", [128, 2 * DR], F32) for i in range(4)]
                tmpd = [sbF2(f"tmpd{i}", [128, DR * 128], I32) for i in range(2)]
                wdq = [sbF2(f"wdq{i}", [128, D], BF16) for i in range(4)]
                wT = [sbF2(f"wTf{i}", [128, DT * 128], BF16) for i in range(4)]
                wT3 = [w.ap().rearrange("p (d o) -> p d o", d=DT) for w in wT]
                sab = [sbF2(f"sab{i}", [128, Q], BF16) for i in range(2)]
                hab = [sbF2(f"hab{i}", [128, Q], BF16) for i in range(2)]
                ps_ab = [psF(f"psab{i}", [128, Q], F32) for i in range(2)]

                pk1_r = pk1.rearrange("(h r) t -> h (r t)", r=DR)
                pk3_r = pk3.rearrange("(h r) t -> h (r t)", r=DR)
                s1_r = s1.rearrange("(h g) -> h g", g=2 * DR)
                s3_r = s3.rearrange("(h g) -> h g", g=2 * DR)
                qsp = _splits(Q, 512)

                dPK = [dch[21], dch[22], dch[19], dch[20]]
                deq_rd = [0, 0, 0, 0]
                pk_rd = [0, 0, 0, 0]
                wT_mm = [0, 0, 0, 0]
                silu_last = 0
                tokh_slot = [0, 0]
                tokh_last = 0
                dHA = [dch[23], dch[3]]

                def load_pk(buf_i, pk_r, s_r, row_t, col0, nh):
                    ch = dPK[buf_i]
                    cv.wait(sy, pk_rd[buf_i])
                    sy.dma_start(pkb[buf_i][:, 0:nh * 64],
                                 pk_r[row_t * 128:(row_t + 1) * 128,
                                      col0 * 64:(col0 + nh) * 64]) \
                        .then_inc(ch.s, 16)
                    sy.dma_start(sclb[buf_i][:, 0:2 * nh],
                                 s_r[row_t * 128:(row_t + 1) * 128,
                                     2 * col0:2 * (col0 + nh)]) \
                        .then_inc(ch.s, 16)
                    ch.n += 32

                def dequant(buf_i, tsl, nh, FSCL):
                    ch = dPK[buf_i]
                    ct.wait(v, deq_rd[buf_i])
                    ch.wait(v)
                    vop(v.tensor_scalar_mul, m8b[buf_i][:, 0:2 * nh],
                        sclb[buf_i][:, 0:2 * nh], -8.0)
                    pk3v = pkb[buf_i].ap() \
                        .rearrange("p (r t) -> p r t", t=64)[:, 0:nh, :]
                    td = tmpd[tsl].ap() \
                        .rearrange("p (r h t) -> p r h t", h=2, t=64)[:, 0:nh]
                    vop(v.tensor_single_scalar, td[:, :, 0, :], pk3v, 4,
                        ALU.arith_shift_right)
                    vop(v.tensor_scalar, td[:, :, 1, :], pk3v, 15, 8,
                        ALU.bitwise_and, ALU.bitwise_xor)
                    wv = wdq[buf_i].ap() \
                        .rearrange("p (r h t) -> p r h t", h=2, t=64)[:, 0:nh]
                    sc_m = bass.AP(sclb[buf_i].ap().tensor, 0,
                                   [[FSCL, 128], [2, nh], [0, 64]])
                    sc_l = bass.AP(sclb[buf_i].ap().tensor, 1,
                                   [[FSCL, 128], [2, nh], [0, 64]])
                    m8_l = bass.AP(m8b[buf_i].ap().tensor, 1,
                                   [[FSCL, 128], [2, nh], [0, 64]])
                    vop(v.tensor_tensor, wv[:, :, 0, :], td[:, :, 0, :], sc_m,
                        ALU.mult)
                    vop(v.tensor_tensor, wv[:, :, 1, :], td[:, :, 1, :], sc_l,
                        ALU.mult)
                    tokd = vop(v.scalar_tensor_tensor, wv[:, :, 1, :],
                               wv[:, :, 1, :], 1.0, m8_l, ALU.mult, ALU.add)
                    return tokd

                srcs = [(pk1_r, s1_r), (pk3_r, s3_r)]

                def deq_tr(ht):
                    for wi in range(2):
                        bi = 2 * (ht % 2) + wi
                        tokd = dequant(bi, wi, DR, 2 * DR)
                        pk_rd[bi] = tokd
                        ct.wait(t, wT_mm[bi])
                        transpose_to(wT3[bi], DT,
                                     lambda dd: wdq[bi][:, dd * 128:(dd + 1) * 128],
                                     ptrb, ptrb_tok, tokd)
                        deq_rd[bi] = ct.n

                for wi in range(2):
                    load_pk(2 * 0 + wi, *srcs[wi], 0, 0, DR)
                if NHT > 1:
                    for wi in range(2):
                        load_pk(2 * 1 + wi, *srcs[wi], 1, 0, DR)
                deq_tr(0)
                for ht in range(NHT):
                    for wi in range(2):
                        bi = 2 * (ht % 2) + wi
                        cv.wait(t, ptrb_tok[0])
                        cv.wait(t, ptrb_tok[1])
                        if wi == 0:
                            cs.wait(t, silu_last)
                        else:
                            cv.wait(t, tokh_last)
                        for no, nl in qsp:
                            for dd in range(DT):
                                ins = t.matmul(ps_ab[wi][:, no:no + nl],
                                               wT3[bi][:, dd, :],
                                               ln2T3[:, dd, no:no + nl],
                                               start=(dd == 0),
                                               stop=(dd == DT - 1))
                        wT_mm[bi] = ct.inc(ins)
                    if ht + 2 < NHT:
                        for wi in range(2):
                            load_pk(2 * ((ht + 2) % 2) + wi, *srcs[wi],
                                    ht + 2, 0, DR)
                    if ht + 1 < NHT:
                        deq_tr(ht + 1)
                    hsl = ht % 2
                    ct.wait(s, wT_mm[2 * hsl + 0])
                    dHA[hsl].wait(s)
                    cv.wait(s, tokh_slot[hsl])
                    silu_last = sop(s.activation, sab[hsl][:, :],
                                    ps_ab[0][:, :], AF.Silu)
                    cs.wait(v, silu_last)
                    ct.wait(v, wT_mm[2 * hsl + 1])
                    dHA[hsl].wait(v)
                    tokh = vop(v.tensor_tensor, hab[hsl][:, :], sab[hsl][:, :],
                               ps_ab[1][:, :], ALU.mult)
                    tokh_slot[hsl] = tokh
                    tokh_last = tokh
                    cv.wait(sy, tokh)
                    sy.dma_start(hact_sc[ht * 128:(ht + 1) * 128, :],
                                 hab[hsl][:, :]).then_inc(dHA[hsl].s, 16)
                    dHA[hsl].n += 16
                barrier()

        with ExitStack() as stG:
            sbG = lambda n, sh, d: stG.enter_context(nc.sbuf_tensor(n, sh, d))
            psG = lambda n, sh, d: stG.enter_context(nc.psum_tensor(n, sh, d))
            HH = HR // 2 if HR % 2 == 0 else HR
            nhalves = HR // HH
            w2T = sbG("w2T", [128, HR * 512], BF16)
            w2T3 = w2T.ap().rearrange("p (h d) -> p h d", h=HR)
            HB = 4
            hbuf = [sbG(f"hbuf{i}", [128, HB * Q], BF16) for i in range(2)]
            QG = min(4, QT)
            ytl = [sbG(f"ytl{i}", [128, 512], F32) for i in range(4)]
            otl = [sbG(f"otl{i}", [128, 512], F32) for i in range(4)]
            pkb = [sbG(f"pkbG{i}", [128, HH * 64], I32) for i in range(2)]
            sclb = [sbG(f"sclbG{i}", [128, 2 * HH], F32) for i in range(2)]
            m8b = [sbG(f"m8bG{i}", [128, 2 * HH], F32) for i in range(2)]
            tmpd = [sbG("tmpdG", [128, HH * 128], I32)]
            wdq = [sbG(f"wdqG{i}", [128, HH * 128], BF16) for i in range(2)]
            ptrb = [psG(f"ptrbG{i}", [128, 8 * 128], BF16) for i in range(2)]
            ps_q = [psG(f"psq{i}", [128, 512], F32) for i in range(QG)]

            pk2_r = pk2.rearrange("(d r) t -> d (r t)", r=HR)
            s2_r = s2.rearrange("(d g) -> d g", g=2 * HR)
            dPK = [dch[21], dch[22]]
            deq_rd = [0, 0]
            ptrb_tok = [0, 0]

            def dequant2(buf_i, row_t, col0, nh):
                ch = dPK[buf_i]
                ct.wait(sy, deq_rd[buf_i])
                sy.dma_start(pkb[buf_i][:, 0:nh * 64],
                             pk2_r[row_t * 128:(row_t + 1) * 128,
                                   col0 * 64:(col0 + nh) * 64]).then_inc(ch.s, 16)
                sy.dma_start(sclb[buf_i][:, 0:2 * nh],
                             s2_r[row_t * 128:(row_t + 1) * 128,
                                  2 * col0:2 * (col0 + nh)]).then_inc(ch.s, 16)
                ch.n += 32
                ch.wait(v)
                vop(v.tensor_scalar_mul, m8b[buf_i][:, 0:2 * nh],
                    sclb[buf_i][:, 0:2 * nh], -8.0)
                pk3v = pkb[buf_i].ap().rearrange("p (r t) -> p r t", t=64)[:, 0:nh]
                td = tmpd[0].ap().rearrange("p (r h t) -> p r h t",
                                            h=2, t=64)[:, 0:nh]
                vop(v.tensor_single_scalar, td[:, :, 0, :], pk3v, 4,
                    ALU.arith_shift_right)
                vop(v.tensor_scalar, td[:, :, 1, :], pk3v, 15, 8,
                    ALU.bitwise_and, ALU.bitwise_xor)
                wv = wdq[buf_i].ap().rearrange("p (r h t) -> p r h t",
                                               h=2, t=64)[:, 0:nh]
                FS = 2 * HH
                sc_m = bass.AP(sclb[buf_i].ap().tensor, 0,
                               [[FS, 128], [2, nh], [0, 64]])
                sc_l = bass.AP(sclb[buf_i].ap().tensor, 1,
                               [[FS, 128], [2, nh], [0, 64]])
                m8_l = bass.AP(m8b[buf_i].ap().tensor, 1,
                               [[FS, 128], [2, nh], [0, 64]])
                vop(v.tensor_tensor, wv[:, :, 0, :], td[:, :, 0, :], sc_m,
                    ALU.mult)
                vop(v.tensor_tensor, wv[:, :, 1, :], td[:, :, 1, :], sc_l,
                    ALU.mult)
                tokd = vop(v.scalar_tensor_tensor, wv[:, :, 1, :],
                           wv[:, :, 1, :], 1.0, m8_l, ALU.mult, ALU.add)
                return tokd

            dH2 = [dch[12], dch[13]]
            dY2 = [dch[14], dch[15], dch[18], dch[23]]
            dO2 = [dch[16], dch[17], dch[8], dch[9]]
            dsp = _splits(D, 512)

            def emit_deq_tr(di, dl):
                ndd = dl // 128
                for i in range(ndd):
                    ddg = di * (512 // 128) + i
                    for half in range(nhalves):
                        bi = (i * nhalves + half) % 2
                        tokd = dequant2(bi, ddg, half * HH, HH)
                        transpose_to(
                            w2T3[:, half * HH:(half + 1) * HH,
                                 i * 128:(i + 1) * 128],
                            HH,
                            lambda hh: wdq[bi][:, hh * 128:(hh + 1) * 128],
                            ptrb, ptrb_tok, tokd)
                        deq_rd[bi] = ct.n
                return cv.n

            hact_r = hact_sc.rearrange("(b p) q -> b p q", p=128)
            h_mm = [0, 0]
            evac_last = 0
            y_rd = [0, 0, 0, 0]
            nhb = (NHT + HB - 1) // HB
            w2_ready = emit_deq_tr(0, dsp[0][1])

            def load_hb(hbi):
                slot = hbi % 2
                ct.wait(sy, h_mm[slot])
                nb = min(HB, NHT - hbi * HB)
                sy.dma_start(
                    hbuf[slot].ap().rearrange("p (j q) -> p j q", q=Q)[:, 0:nb, :],
                    hact_r[hbi * HB:hbi * HB + nb, :, :].transpose([1, 0, 2])) \
                    .then_inc(dH2[slot].s, 16)
                dH2[slot].n += 16

            for di, (do, dl) in enumerate(dsp):
                for q0 in range(0, QT, QG):
                    qn = min(QG, QT - q0)
                    for qi in range(qn):
                        cv.wait(sy, y_rd[qi])
                        sy.dma_start(
                            ytl[qi][:, 0:dl],
                            y_sc[(q0 + qi) * 128:(q0 + qi + 1) * 128,
                                 do:do + dl]).then_inc(dY2[qi].s, 16)
                        dY2[qi].n += 16
                    load_hb(0)
                    for hbi in range(nhb):
                        if hbi + 1 < nhb:
                            load_hb(hbi + 1)
                        slot = hbi % 2
                        dH2[slot].wait(t)
                        if hbi == 0:
                            t.wait_ge(cv.s, w2_ready)
                            cv.wait(t, evac_last)
                        hb3 = hbuf[slot].ap() \
                            .rearrange("p (j q) -> p j q", q=Q)
                        for j in range(min(HB, NHT - hbi * HB)):
                            hh = hbi * HB + j
                            for qi in range(qn):
                                ins = t.matmul(
                                    ps_q[qi][:, 0:dl],
                                    hb3[:, j, (q0 + qi) * 128:(q0 + qi + 1) * 128],
                                    w2T3[:, hh, 0:dl],
                                    start=(hh == 0), stop=(hh == HR - 1))
                        h_mm[slot] = ct.inc(ins)
                    tokM = h_mm[(nhb - 1) % 2]
                    for qi in range(qn):
                        ct.wait(v, tokM)
                        dY2[qi].wait(v)
                        dO2[qi].wait(v)
                        tokO = vop(v.tensor_tensor, otl[qi][:, 0:dl],
                                   ps_q[qi][:, 0:dl], ytl[qi][:, 0:dl], ALU.add)
                        y_rd[qi] = tokO
                        evac_last = tokO
                        cv.wait(sy, tokO)
                        sy.dma_start(
                            out_d[(q0 + qi) * 128:(q0 + qi + 1) * 128,
                                  do:do + dl],
                            otl[qi][:, 0:dl]).then_inc(dO2[qi].s, 16)
                        dO2[qi].n += 16
                if di + 1 < len(dsp):
                    w2_ready = emit_deq_tr(di + 1, dsp[di + 1][1])
            for ch_ in dch:
                if ch_.n:
                    ch_.wait(sy)
            for ch_ in gch:
                if ch_.n:
                    ch_.wait(g)
    return nc


def _rope_tables(n, qoff=0):
    hd = 128
    inv = 1.0 / (THETA ** (np.arange(0, hd, 2, dtype=np.float64) / hd))
    pos = np.arange(qoff, qoff + n, dtype=np.float64)
    fr = np.outer(pos, inv)
    emb = np.concatenate([fr, fr], axis=1)
    cos = np.cos(emb).T.astype(np.float32)
    sinm = np.sin(emb).T.astype(np.float32)
    sinm[:64] *= -1.0
    return np.ascontiguousarray(cos), np.ascontiguousarray(sinm)


def make_core_inputs(c: Cfg, core, x, wdict):
    b = core // c.cpb
    j = core % c.cpb
    qoff = j * c.Q
    cos, sinm = _rope_tables(c.S, 0)
    cosq, sinmq = _rope_tables(c.Q, qoff)
    mask = (np.arange(c.S)[:, None] <= (qoff + np.arange(c.Q))[None, :])
    d = {
        "x_full": np.ascontiguousarray(x[b]),
        "x_q": np.ascontiguousarray(x[b, qoff:qoff + c.Q]),
        "cosk": cos, "sinmk": sinm, "cosq": cosq, "sinmq": sinmq,
        "maskT": mask.astype(ml_dtypes.bfloat16),
        "identb": np.eye(128, dtype=ml_dtypes.bfloat16),
        "onesf": np.ones((1, 128), dtype=np.float32),
        "onesb": np.ones((128, 1), dtype=ml_dtypes.bfloat16),
        "constv": np.tile(np.array([[EPS, -0.5, c.SCALE, -CEXP]], np.float32),
                          (128, 1)),
    }
    d.update(wdict)
    return d


def make_wdict(L0_attn_w_q, L0_attn_w_k, L0_attn_w_v, L0_attn_w_o, L0_attn_w_lq,
               L0_attn_w_lk, L0_attn_w_lv, L0_attn_w_lo, L0_ffn_w1, L0_ffn_s1,
               L0_ffn_w2, L0_ffn_s2, L0_ffn_w3, L0_ffn_s3):
    return {
        "w_q": np.ascontiguousarray(np.asarray(L0_attn_w_q, np.float32)),
        "w_k": np.ascontiguousarray(np.asarray(L0_attn_w_k, np.float32)),
        "w_v": np.ascontiguousarray(np.asarray(L0_attn_w_v, np.float32)),
        "w_o": np.ascontiguousarray(np.asarray(L0_attn_w_o, np.float32)),
        "w_lq": np.ascontiguousarray(np.asarray(L0_attn_w_lq, np.float32)),
        "w_lk": np.ascontiguousarray(np.asarray(L0_attn_w_lk, np.float32)),
        "w_lv": np.ascontiguousarray(np.asarray(L0_attn_w_lv, np.float32)),
        "w_lo": np.ascontiguousarray(np.asarray(L0_attn_w_lo, np.float32)),
        "pk1": np.ascontiguousarray(np.asarray(L0_ffn_w1, np.int32)),
        "s1": np.ascontiguousarray(np.asarray(L0_ffn_s1, np.float32)),
        "pk2": np.ascontiguousarray(np.asarray(L0_ffn_w2, np.int32)),
        "s2": np.ascontiguousarray(np.asarray(L0_ffn_s2, np.float32)),
        "pk3": np.ascontiguousarray(np.asarray(L0_ffn_w3, np.int32)),
        "s3": np.ascontiguousarray(np.asarray(L0_ffn_s3, np.float32)),
    }


def kernel(L0_attn_hidden_states, L0_attn_w_q, L0_attn_w_k, L0_attn_w_v,
           L0_attn_w_o, L0_attn_w_lq, L0_attn_w_lk, L0_attn_w_lv, L0_attn_w_lo,
           L0_ffn_w1, L0_ffn_s1, L0_ffn_w2, L0_ffn_s2, L0_ffn_w3, L0_ffn_s3):
    c = Cfg()
    x = np.asarray(L0_attn_hidden_states, dtype=np.float32)
    wdict = make_wdict(L0_attn_w_q, L0_attn_w_k, L0_attn_w_v, L0_attn_w_o,
                       L0_attn_w_lq, L0_attn_w_lk, L0_attn_w_lv, L0_attn_w_lo,
                       L0_ffn_w1, L0_ffn_s1, L0_ffn_w2, L0_ffn_s2, L0_ffn_w3,
                       L0_ffn_s3)
    nc = build_program(c)
    in_maps = [make_core_inputs(c, core, x, wdict) for core in range(c.ncores)]
    res = run_bass_kernel_spmd(nc, in_maps, core_ids=list(range(c.ncores)))
    out = np.empty((c.B, c.S, c.D), dtype=np.float32)
    for core in range(c.ncores):
        b, j = core // c.cpb, core % c.cpb
        out[b, j * c.Q:(j + 1) * c.Q] = res.results[core]["out"]
    return out


# version 04 (final)
# speedup vs baseline: 1.1749x; 1.0226x over previous
import math
from contextlib import ExitStack

import numpy as np
import ml_dtypes

import concourse.bass as bass
import concourse.mybir as mybir
from concourse.bass_utils import run_bass_kernel_spmd

F32 = mybir.dt.float32
BF16 = mybir.dt.bfloat16
I32 = mybir.dt.int32
AF = mybir.ActivationFunctionType
ALU = mybir.AluOpType
AX = mybir.AxisListType

EPS = 1e-5
THETA = 10000.0
CEXP = 24.0


class Cfg:
    def __init__(self, D=4096, S=4096, Q=1024, H=32, KVH=8, LAT=1024, NHID=11008,
                 ncores=8, B=2):
        self.D, self.S, self.Q, self.H, self.KVH, self.LAT, self.NHID = \
            D, S, Q, H, KVH, LAT, NHID
        self.ncores, self.B = ncores, B
        self.HD = 128
        self.DT = D // 128
        self.LT = LAT // 128
        self.KT = S // 128
        self.QT = Q // 128
        self.NHT = NHID // 128
        self.DR = D // 128
        self.HR = NHID // 128
        self.RC = min(1024, S)
        self.NRC = S // self.RC
        self.G = H // KVH
        self.SCALE = 1.0 / math.sqrt(self.HD)
        self.cpb = ncores // B


def _splits(total, step):
    return [(o, min(step, total - o)) for o in range(0, total, step)]


class Ch:

    def __init__(self, nc, stack, name):
        self.s = stack.enter_context(nc.semaphore(name))
        self.n = 0

    def inc(self, instr, k=1):
        instr.then_inc(self.s, k)
        self.n += k
        return self.n

    def wait(self, eng, n=None):
        eng.wait_ge(self.s, self.n if n is None else n)


def build_program(c: Cfg):
    nc = bass.Bass("TRN2", target_bir_lowering=False)
    D, S, Q, H, KVH, LAT, NHID = c.D, c.S, c.Q, c.H, c.KVH, c.LAT, c.NHID
    DT, LT, KT, QT, NHT, DR, HR = c.DT, c.LT, c.KT, c.QT, c.NHT, c.DR, c.HR
    RC, NRC, G = c.RC, c.NRC, c.G

    dtt = nc.dram_tensor
    xf = dtt("x_full", [S, D], F32, kind="ExternalInput").ap()
    xq = dtt("x_q", [Q, D], F32, kind="ExternalInput").ap()
    w_q = dtt("w_q", [D, H * 128], F32, kind="ExternalInput").ap()
    w_k = dtt("w_k", [D, KVH * 128], F32, kind="ExternalInput").ap()
    w_v = dtt("w_v", [D, KVH * 128], F32, kind="ExternalInput").ap()
    w_o = dtt("w_o", [H * 128, D], F32, kind="ExternalInput").ap()
    w_lq = dtt("w_lq", [D, LAT], F32, kind="ExternalInput").ap()
    w_lk = dtt("w_lk", [LAT, H * 128], F32, kind="ExternalInput").ap()
    w_lv = dtt("w_lv", [D, H * 128], F32, kind="ExternalInput").ap()
    w_lo = dtt("w_lo", [H * 128, D], F32, kind="ExternalInput").ap()
    pk1 = dtt("pk1", [NHID * DR, 64], I32, kind="ExternalInput").ap()
    s1 = dtt("s1", [NHID * DR * 2], F32, kind="ExternalInput").ap()
    pk2 = dtt("pk2", [D * HR, 64], I32, kind="ExternalInput").ap()
    s2 = dtt("s2", [D * HR * 2], F32, kind="ExternalInput").ap()
    pk3 = dtt("pk3", [NHID * DR, 64], I32, kind="ExternalInput").ap()
    s3 = dtt("s3", [NHID * DR * 2], F32, kind="ExternalInput").ap()
    cosk = dtt("cosk", [128, S], F32, kind="ExternalInput").ap()
    sinmk = dtt("sinmk", [128, S], F32, kind="ExternalInput").ap()
    cosq = dtt("cosq", [128, Q], F32, kind="ExternalInput").ap()
    sinmq = dtt("sinmq", [128, Q], F32, kind="ExternalInput").ap()
    maskT = dtt("maskT", [S, Q], BF16, kind="ExternalInput").ap()
    identb = dtt("identb", [128, 128], BF16, kind="ExternalInput").ap()
    onesf = dtt("onesf", [1, 128], F32, kind="ExternalInput").ap()
    onesb = dtt("onesb", [128, 1], BF16, kind="ExternalInput").ap()
    constv = dtt("constv", [128, 4], F32, kind="ExternalInput").ap()
    out_d = dtt("out", [Q, D], F32, kind="ExternalOutput").ap()

    k_sc = dtt("k_sc", [KVH * 128, S], BF16, kind="Internal").ap()
    v_sc = dtt("v_sc", [128, KVH * S], BF16, kind="Internal").ap()
    lk_sc = dtt("lk_sc", [H * 128, S], BF16, kind="Internal").ap()
    lv_sc = dtt("lv_sc", [128, H * S], BF16, kind="Internal").ap()
    q_sc = dtt("q_sc", [H * 128, Q], BF16, kind="Internal").ap()
    ctx_sc = dtt("ctx_sc", [2 * H * 128, Q], BF16, kind="Internal").ap()
    x1a_sc = dtt("x1a_sc", [Q, D], F32, kind="Internal").ap()
    y_sc = dtt("y_sc", [Q, D], F32, kind="Internal").ap()
    hact_sc = dtt("hact_sc", [NHT * 128, Q], BF16, kind="Internal").ap()

    sy, g, t, v, s = nc.sync, nc.gpsimd, nc.tensor, nc.vector, nc.scalar

    with ExitStack() as st0:
        idt = st0.enter_context(nc.sbuf_tensor("idt", [128, 128], BF16))
        cb = st0.enter_context(nc.sbuf_tensor("cb", [128, 4], F32))
        onef = st0.enter_context(nc.sbuf_tensor("onef", [1, 128], F32))
        oneb = st0.enter_context(nc.sbuf_tensor("oneb", [128, 1], BF16))

        cv = Ch(nc, st0, "cv")
        cs = Ch(nc, st0, "cs")
        ct = Ch(nc, st0, "ct")
        dch = [Ch(nc, st0, f"d{i}") for i in range(24)]
        gch = [Ch(nc, st0, f"g{i}") for i in range(4)]

        blk = st0.enter_context(nc.Block())

        def vop(fn, *a, **kw):
            v.wait_ge(cv.s, cv.n)
            return cv.inc(fn(*a, **kw))

        def sop(fn, *a, **kw):
            s.wait_ge(cs.s, cs.n)
            return cs.inc(fn(*a, **kw))

        def barrier():
            for ch_ in dch:
                if ch_.n:
                    ch_.wait(sy)
            for ch_ in gch:
                if ch_.n:
                    ch_.wait(g)
            nc.all_engine_barrier()

        ch0 = dch[0]
        sy.dma_start(idt[:, :], identb).then_inc(ch0.s, 16); ch0.n += 16
        sy.dma_start(onef[:, :], onesf).then_inc(ch0.s, 16); ch0.n += 16
        sy.dma_start(oneb[:, :], onesb).then_inc(ch0.s, 16); ch0.n += 16
        sy.dma_start(cb[:, :], constv).then_inc(ch0.s, 16); ch0.n += 16
        ch0.wait(t)
        ch0.wait(v)
        ch0.wait(s)

        def transpose_to(dst3, n_tiles, src_ap_fn, ptrb, ptrb_tok, src_cv_tok):
            if src_cv_tok:
                cv.wait(t, src_cv_tok)
            last = 0
            for gi, (g0, gl) in enumerate(_splits(n_tiles, 8)):
                bsel = gi % 2
                cv.wait(t, ptrb_tok[bsel])
                for j in range(gl):
                    ins = t.transpose(ptrb[bsel][:, j * 128:(j + 1) * 128],
                                      src_ap_fn(g0 + j), idt[:, :])
                tokt = ct.inc(ins)
                ct.wait(v, tokt)
                srcv = ptrb[bsel].ap().rearrange("p (j k) -> p j k", k=128)[:, 0:gl, :]
                ptrb_tok[bsel] = vop(v.tensor_copy, dst3[:, g0:g0 + gl, :], srcv)
                last = ptrb_tok[bsel]
            return last

        def ln_rows(src_ap, xt, sq, lnbf, stats, ld_ch, war_cv, war_cs, extra=None):
            cv.wait(sy, war_cv)
            cs.wait(sy, war_cs)
            sy.dma_start(xt[:, :], src_ap).then_inc(ld_ch.s, 16)
            ld_ch.n += 16
            if extra is not None:
                extra()
            nsum, ssq, m2, varv, rstd = stats
            ld_ch.wait(s)
            s.wait_ge(cv.s, cv.n)
            toksq = sop(s.activation, sq[:, :], xt[:, :], AF.Square,
                        accum_out=ssq[:, :])
            ld_ch.wait(v)
            vop(v.tensor_reduce, nsum[:, :], xt[:, :], AX.X, ALU.add)
            vop(v.tensor_scalar_mul, nsum[:, :], nsum[:, :], -1.0 / D)
            vop(v.tensor_tensor, m2[:, :], nsum[:, :], nsum[:, :], ALU.mult)
            cs.wait(v, toksq)
            vop(v.tensor_scalar_mul, varv[:, :], ssq[:, :], 1.0 / D)
            tokv = vop(v.tensor_tensor, varv[:, :], varv[:, :], m2[:, :],
                       ALU.subtract)
            cv.wait(s, tokv)
            sop(s.activation, varv[:, :], varv[:, :], AF.Ln, bias=cb[:, 0:1])
            toks = sop(s.activation, rstd[:, :], varv[:, :], AF.Exp, scale=cb[:, 1:2])
            cs.wait(v, toks)
            tokl = vop(v.tensor_scalar, lnbf[:, :], xt[:, :],
                       nsum[:, 0:1], rstd[:, 0:1], ALU.add, ALU.mult)
            return tokl, toksq

        with ExitStack() as stA:
            sbA = lambda n, sh, d: stA.enter_context(nc.sbuf_tensor(n, sh, d))
            psA = lambda n, sh, d: stA.enter_context(nc.psum_tensor(n, sh, d))
            lnT = sbA("lnT", [128, DT * RC], BF16)
            lnT3 = lnT.ap().rearrange("p (d r) -> p d r", d=DT)
            lqT = sbA("lqT", [128, LT * RC], BF16)
            lqT3 = lqT.ap().rearrange("p (l r) -> p l r", l=LT)
            xt = [sbA(f"xt{i}", [128, D], F32) for i in range(2)]
            sqj = sbA("sqj", [128, D], F32)
            lnbf = sbA("lnbf", [128, D], BF16)
            stats = [sbA(f"st{i}", [128, 1], F32) for i in range(5)]
            wt_bf = [sbA(f"wtbf{i}", [128, DT * 128], BF16) for i in range(2)]
            wt3 = [w.ap().rearrange("p (d o) -> p d o", d=DT) for w in wt_bf]
            wlk_bf = [sbA(f"wlkbf{i}", [128, LT * 128], BF16) for i in range(2)]
            wlk3 = [w.ap().rearrange("p (l o) -> p l o", l=LT) for w in wlk_bf]
            cosc = sbA("cosc", [128, RC], F32)
            sinc = sbA("sinc", [128, RC], F32)
            rt1 = sbA("rt1", [128, RC], F32)
            rt2 = sbA("rt2", [128, RC], F32)
            obf = [sbA(f"obf{i}", [128, RC], BF16) for i in range(3)]
            vtb = [sbA(f"vtb{i}", [128, RC], BF16) for i in range(2)]
            ptrb = [psA(f"ptrb{i}", [128, 8 * 128], BF16) for i in range(2)]
            pso = [psA(f"pso{i}", [128, min(512, RC)], F32) for i in range(2)]

            ch1 = dch[1]

            xf3 = xf.rearrange("(a p) d -> a p d", p=128)
            xq3 = xq.rearrange("(a p) d -> a p d", p=128)
            wq_r = w_q.rearrange("(a b) c -> b a c", b=128)
            wk_r = w_k.rearrange("(a b) c -> b a c", b=128)
            wv_r = w_v.rearrange("(a b) c -> b a c", b=128)
            wlq_r = w_lq.rearrange("(a b) c -> b a c", b=128)
            wlv_r = w_lv.rearrange("(a b) c -> b a c", b=128)
            wlk_r = w_lk.rearrange("(a b) c -> b a c", b=128)

            ptrb_tok = [0, 0]

            def proj_chunk(is_q, rc_idx, nrows):
                r0 = rc_idx * RC
                nrt = nrows // 128
                if is_q:
                    sy.dma_start(cosc[:, 0:Q], cosq).then_inc(ch1.s, 16)
                    sy.dma_start(sinc[:, 0:Q], sinmq).then_inc(ch1.s, 16)
                else:
                    sy.dma_start(cosc[:, :], cosk[:, r0:r0 + RC]) \
                        .then_inc(ch1.s, 16)
                    sy.dma_start(sinc[:, :], sinmk[:, r0:r0 + RC]) \
                        .then_inc(ch1.s, 16)
                ch1.n += 32
                ch1.wait(v)
                xt_war = [(0, 0), (0, 0)]
                for ti in range(nrt):
                    bs = ti % 2
                    src = (xq3 if is_q else xf3)[(r0 // 128) + ti, :, :]
                    tokl, toksq = ln_rows(src, xt[bs], sqj, lnbf, stats,
                                          dch[2 + bs], *xt_war[bs])
                    xt_war[bs] = (tokl, toksq)
                    transpose_to(lnT3[:, :, ti * 128:(ti + 1) * 128], DT,
                                 lambda dd: lnbf[:, dd * 128:(dd + 1) * 128],
                                 ptrb, ptrb_tok, tokl)
                lnT_ready = cv.n

                if is_q:
                    wlist = [("q", wq_r, hh) for hh in range(H)]
                else:
                    wlist = ([("k", wk_r, hh) for hh in range(KVH)]
                             + [("v", wv_r, hh) for hh in range(KVH)]
                             + [("lq", wlq_r, oc) for oc in range(LT)]
                             + [("lv", wlv_r, hh) for hh in range(H)])
                cos_t, sin_t = cosc, sinc
                cpos0 = 0
                nfs = _splits(nrows, 512)

                wt_last_mm = [0, 0]
                pso_tok = [0, 0]
                obf_rd = [None, None, None]
                vtb_rd = [None, None]

                def wait_reader(eng, rd):
                    if rd is None:
                        return
                    kind, a, b = rd
                    if kind == "cv":
                        cv.wait(eng, a)
                    elif kind == "ct":
                        ct.wait(eng, a)
                    else:
                        a.wait(eng, b)

                def emit_proj(wi, ty, wr, oc, lq_ready):
                    nonlocal ptrb_tok
                    ws = wi % 2
                    gc = gch[ws]
                    ct.wait(g, wt_last_mm[ws])
                    if ty == "lk":
                        g.dma_start(wlk_bf[ws][:, :],
                                    wr[:, :, oc * 128:(oc + 1) * 128]) \
                            .then_inc(gc.s, 16)
                        lhs3, nlhs, rhs3, rhs_rdy = wlk3[ws], LT, lqT3, lq_ready
                    else:
                        g.dma_start(wt_bf[ws][:, :],
                                    wr[:, :, oc * 128:(oc + 1) * 128]) \
                            .then_inc(gc.s, 16)
                        lhs3, nlhs, rhs3, rhs_rdy = wt3[ws], DT, lnT3, lnT_ready
                    gc.n += 16
                    ob_i = wi % 3
                    ob = obf[ob_i]
                    rope_toks = []
                    for nf, (no, nl) in enumerate(nfs):
                        po = (wi * len(nfs) + nf) % 2
                        cv.wait(t, pso_tok[po])
                        gc.wait(t)
                        t.wait_ge(cv.s, rhs_rdy)
                        for dd in range(nlhs):
                            ins = t.matmul(pso[po][:, 0:nl], lhs3[:, dd, :],
                                           rhs3[:, dd, no:no + nl],
                                           start=(dd == 0), stop=(dd == nlhs - 1))
                        mm_tok = ct.inc(ins)
                        wt_last_mm[ws] = mm_tok
                        rows = slice(no, no + nl)
                        ct.wait(v, mm_tok)
                        if nf == 0:
                            wait_reader(v, obf_rd[ob_i])
                        if ty in ("k", "lk", "q"):
                            cp = cos_t[:, cpos0 + no:cpos0 + no + nl]
                            sp = sin_t[:, cpos0 + no:cpos0 + no + nl]
                            vop(v.tensor_tensor, rt1[:, rows], pso[po][:, 0:nl],
                                cp, ALU.mult)
                            vop(v.tensor_tensor, rt2[0:64, rows],
                                pso[po][64:128, 0:nl], sp[0:64, :], ALU.mult)
                            vop(v.tensor_tensor, rt2[64:128, rows],
                                pso[po][0:64, 0:nl], sp[64:128, :], ALU.mult)
                            tokr = vop(v.tensor_tensor, ob[:, rows],
                                       rt1[:, rows], rt2[:, rows], ALU.add)
                        else:
                            tokr = vop(v.tensor_copy, ob[:, rows],
                                       pso[po][:, 0:nl])
                        pso_tok[po] = tokr
                        rope_toks.append(tokr)
                    if ty in ("k", "lk", "q"):
                        c_st = dch[6 + ob_i]
                        cv.wait(sy, rope_toks[-1])
                        dst = {"k": k_sc, "lk": lk_sc, "q": q_sc}[ty]
                        if ty == "q":
                            tgt = dst[oc * 128:(oc + 1) * 128, 0:nrows]
                        else:
                            tgt = dst[oc * 128:(oc + 1) * 128, r0:r0 + nrows]
                        sy.dma_start(tgt, ob[:, 0:nrows]).then_inc(c_st.s, 16)
                        c_st.n += 16
                        obf_rd[ob_i] = ("dma", c_st, c_st.n)
                    elif ty == "lq":
                        pass
                    if ty in ("v", "lv"):
                        vs = wi % 2
                        wait_reader(v, vtb_rd[vs])
                        vt3 = vtb[vs].ap().rearrange("p (j k) -> p j k", k=128)
                        tokT = transpose_to(vt3, nrt,
                                            lambda j: ob[:, j * 128:(j + 1) * 128],
                                            ptrb, ptrb_tok, rope_toks[-1])
                        obf_rd[ob_i] = ("ct", ct.n, None)
                        c_st = dch[9 + vs]
                        cv.wait(sy, tokT)
                        dst = v_sc if ty == "v" else lv_sc
                        dst_r = dst[:, oc * S + r0:oc * S + r0 + nrows]
                        sy.dma_start(dst_r, vtb[vs][:, 0:nrows]).then_inc(c_st.s, 16)
                        c_st.n += 16
                        vtb_rd[vs] = ("dma", c_st, c_st.n)

                def emit_lq(wi, oc):
                    ws = wi % 2
                    gc = gch[ws]
                    ct.wait(g, wt_last_mm[ws])
                    g.dma_start(wt_bf[ws][:, :],
                                wlq_r[:, :, oc * 128:(oc + 1) * 128]) \
                        .then_inc(gc.s, 16)
                    gc.n += 16
                    for nf, (no, nl) in enumerate(nfs):
                        po = (wi * len(nfs) + nf) % 2
                        cv.wait(t, pso_tok[po])
                        gc.wait(t)
                        t.wait_ge(cv.s, lnT_ready)
                        for dd in range(DT):
                            ins = t.matmul(pso[po][:, 0:nl], wt3[ws][:, dd, :],
                                           lnT3[:, dd, no:no + nl],
                                           start=(dd == 0), stop=(dd == DT - 1))
                        mm_tok = ct.inc(ins)
                        wt_last_mm[ws] = mm_tok
                        ct.wait(v, mm_tok)
                        pso_tok[po] = vop(v.tensor_copy,
                                          lqT3[:, oc, no:no + nl],
                                          pso[po][:, 0:nl])

                wi = 0
                for ty, wr, oc in wlist:
                    if ty == "lq":
                        emit_lq(wi, oc)
                    else:
                        emit_proj(wi, ty, wr, oc, 0)
                    wi += 1
                if not is_q:
                    lq_ready = cv.n
                    for oc in range(H):
                        emit_proj(wi, "lk", wlk_r, oc, lq_ready)
                        wi += 1
                barrier()

            for rc_idx in range(NRC):
                proj_chunk(False, rc_idx, RC)
            proj_chunk(True, 0, Q)
            barrier()

        with ExitStack() as stB:
            sbB = lambda n, sh, d: stB.enter_context(nc.sbuf_tensor(n, sh, d))
            psB = lambda n, sh, d: stB.enter_context(nc.psum_tensor(n, sh, d))
            maskb = sbB("maskb", [128, KT * Q], BF16)
            mask3 = maskb.ap().rearrange("p (k q) -> p k q", k=KT)
            kbuf = [sbB(f"kb{i}", [128, S], BF16) for i in range(2)]
            vbuf = [sbB(f"vb{i}", [128, KT * 128], BF16) for i in range(2)]
            lkbuf = [sbB(f"lkb{i}", [128, S], BF16) for i in range(2)]
            lvbuf = [sbB(f"lvb{i}", [128, KT * 128], BF16) for i in range(2)]
            qbuf = [sbB(f"qb{i}", [128, Q], BF16) for i in range(2)]
            pmb = [sbB(f"pm{i}", [128, Q], BF16) for i in range(3)]
            ctxb = [sbB(f"ctx{i}", [128, Q], BF16) for i in range(2)]
            recipb = sbB("recipb", [1, Q], F32)
            rbb = sbB("rbb", [128, Q], F32)
            ps_s = [psB(f"pss{i}", [128, Q], F32) for i in range(2)]
            ps_ctx = psB("psctx", [128, Q], F32)
            ps_aux = psB("psaux", [128, Q], F32)

            chm = dch[11]
            sy.dma_start(mask3, maskT.rearrange("(k p) q -> p k q", p=128)) \
                .then_inc(chm.s, 16)
            chm.n += 16
            chm.wait(v)
            chm.wait(s)

            dKV = [dch[12], dch[13]]
            dLTc = [dch[14], dch[15]]
            dCX = [dch[16], dch[17]]
            kv_tok = {}
            done_tok = {}

            def load_kv(hg):
                slot = hg % 2
                ct.wait(sy, done_tok.get(("kv", hg - 2), 0))
                ch = dKV[slot]
                sy.dma_start(kbuf[slot][:, :],
                             k_sc[hg * 128:(hg + 1) * 128, :]).then_inc(ch.s, 16)
                sy.dma_start(vbuf[slot][:, :],
                             v_sc[:, hg * S:(hg + 1) * S]).then_inc(ch.s, 16)
                ch.n += 32
                kv_tok[("kv", hg)] = ch.n

            def load_lat(h):
                slot = h % 2
                ct.wait(sy, done_tok.get(("lt", h - 2), 0))
                ch = dLTc[slot]
                sy.dma_start(lkbuf[slot][:, :],
                             lk_sc[h * 128:(h + 1) * 128, :]).then_inc(ch.s, 16)
                sy.dma_start(lvbuf[slot][:, :],
                             lv_sc[:, h * S:(h + 1) * S]).then_inc(ch.s, 16)
                sy.dma_start(qbuf[slot][:, :],
                             q_sc[h * 128:(h + 1) * 128, :]).then_inc(ch.s, 16)
                ch.n += 48
                kv_tok[("lt", h)] = ch.n

            load_kv(0)
            load_lat(0)

            exp_tok = [0] * KT
            pm_tok = [0] * KT
            pv_tok = [0] * KT
            ctx_dma = [0, 0]
            tokRB_prev = 0
            qsp = _splits(Q, 512)

            for h in range(H):
                if h + 1 < H:
                    if (h + 1) % G == 0:
                        load_kv((h + 1) // G)
                    load_lat(h + 1)
                for a in range(2):
                    lslot = h % 2
                    if a == 0:
                        kvs = (h // G) % 2
                        kb_, vb_ = kbuf[kvs], vbuf[kvs]
                        dKV[kvs].wait(t, kv_tok[("kv", h // G)])
                    else:
                        kb_, vb_ = lkbuf[lslot], lvbuf[lslot]
                    dLTc[lslot].wait(t, kv_tok[("lt", h)])
                    qb_ = qbuf[lslot]
                    vb3 = vb_.ap().rearrange("p (k j) -> p k j", j=128)
                    for kt in range(KT + 2):
                        if kt < KT:
                            sl = kt % 2
                            pm = pmb[kt % 3]
                            cs.wait(t, exp_tok[(kt - 2) % KT])
                            for qo, ql in qsp:
                                ins = t.matmul(ps_s[sl][:, qo:qo + ql],
                                               kb_[:, kt * 128:(kt + 1) * 128],
                                               qb_[:, qo:qo + ql],
                                               start=True, stop=True)
                            tokS = ct.inc(ins)
                            ct.wait(s, tokS)
                            ct.wait(s, pv_tok[(kt - 3) % KT])
                            s.wait_ge(cs.s, cs.n)
                            exp_tok[kt] = cs.inc(s.activation(
                                pm[:, :], ps_s[sl][:, :], AF.Exp,
                                scale=cb[:, 2:3], bias=cb[:, 3:4]))
                            cs.wait(v, exp_tok[kt])
                            pm_tok[kt] = vop(v.tensor_tensor, pm[:, :],
                                             pm[:, :], mask3[:, kt, :],
                                             ALU.mult)
                        if kt >= 2:
                            k2 = kt - 2
                            pm2 = pmb[k2 % 3]
                            cv.wait(t, pm_tok[k2])
                            if k2 == 0:
                                cv.wait(t, tokRB_prev)
                            for qo, ql in qsp:
                                t.matmul(ps_aux[0:1, qo:qo + ql], oneb[:, :],
                                         pm2[:, qo:qo + ql],
                                         start=(k2 == 0), stop=(k2 == KT - 1))
                                ins = t.matmul(ps_ctx[:, qo:qo + ql],
                                               vb3[:, k2, :],
                                               pm2[:, qo:qo + ql],
                                               start=(k2 == 0), stop=(k2 == KT - 1))
                            pv_tok[k2] = ct.inc(ins)
                    ct.wait(v, pv_tok[KT - 1])
                    tokR = vop(v.reciprocal, recipb[:, :], ps_aux[0:1, :])
                    cv.wait(t, tokR)
                    for qo, ql in qsp:
                        ins = t.matmul(ps_aux[:, qo:qo + ql], onef[:, :],
                                       recipb[0:1, qo:qo + ql],
                                       start=True, stop=True)
                    tokB = ct.inc(ins)
                    ct.wait(v, tokB)
                    tokRB = vop(v.tensor_copy, rbb[:, :], ps_aux[:, :])
                    tokRB_prev = tokRB
                    cslot = a
                    dCX[cslot].wait(v, ctx_dma[cslot])
                    tokC = vop(v.tensor_tensor, ctxb[cslot][:, :], ps_ctx[:, :],
                               rbb[:, :], ALU.mult)
                    cv.wait(sy, tokC)
                    sy.dma_start(
                        ctx_sc[(a * H + h) * 128:(a * H + h + 1) * 128, :],
                        ctxb[cslot][:, :]).then_inc(dCX[cslot].s, 16)
                    dCX[cslot].n += 16
                    ctx_dma[cslot] = dCX[cslot].n
                    cv.wait(t, tokC)
                    if a == 1:
                        done_tok[("kv", h // G)] = ct.n
                        done_tok[("lt", h)] = ct.n
            barrier()

        with ExitStack() as stO:
            sbO = lambda n, sh, d: stO.enter_context(nc.sbuf_tensor(n, sh, d))
            psO = lambda n, sh, d: stO.enter_context(nc.psum_tensor(n, sh, d))
            ctxa = sbO("ctxa", [128, H * Q], BF16)
            ctxa3 = ctxa.ap().rearrange("p (h q) -> p h q", h=H)
            wob = [sbO(f"wob{i}", [128, H * 512], BF16) for i in range(2)]
            wob3 = [w.ap().rearrange("p (h d) -> p h d", h=H) for w in wob]
            o_bf = [sbO(f"o_bf{i}", [128, 512], F32) for i in range(2)]
            x1t = [sbO(f"x1t{i}", [128, 512], F32) for i in range(2)]
            ps_op = [psO(f"psop{i}", [128, 512], F32) for i in range(2)]

            dsp = _splits(D, 512)
            for p_i, (wsrc, dst_sc) in enumerate([(w_o, x1a_sc), (w_lo, y_sc)]):
                wsrc_r = wsrc.rearrange("(a b) c -> b a c", b=128)
                cca = dch[18]
                sy.dma_start(ctxa3,
                             ctx_sc[p_i * H * 128:(p_i + 1) * H * 128, :]
                             .rearrange("(h p) q -> p h q", p=128)) \
                    .then_inc(cca.s, 16)
                cca.n += 16
                cca.wait(t)
                wo_mm = [0, 0]
                ps_tok = [0, 0]
                dXc = [dch[12], dch[13]]
                dOc = [dch[19], dch[20]]
                x1_rd = [0, 0]
                for di, (do, dl) in enumerate(dsp):
                    wslot = di % 2
                    gc = gch[2 + wslot]
                    ct.wait(g, wo_mm[wslot])
                    g.dma_start(wob[wslot][:, 0:H * dl],
                                wsrc_r[:, :, do:do + dl]).then_inc(gc.s, 16)
                    gc.n += 16
                    for qsub in range(QT):
                        pslot = qsub % 2
                        bsel = (di * QT + qsub) % 2
                        if p_i == 1:
                            cv.wait(sy, x1_rd[bsel])
                            sy.dma_start(
                                x1t[bsel][:, 0:dl],
                                x1a_sc[qsub * 128:(qsub + 1) * 128, do:do + dl]) \
                                .then_inc(dXc[bsel].s, 16)
                            dXc[bsel].n += 16
                        cv.wait(t, ps_tok[pslot])
                        gc.wait(t)
                        for ht in range(H):
                            ins = t.matmul(
                                ps_op[pslot][:, 0:dl],
                                ctxa3[:, ht, qsub * 128:(qsub + 1) * 128],
                                wob3[wslot][:, ht, 0:dl],
                                start=(ht == 0), stop=(ht == H - 1))
                        tokM = ct.inc(ins)
                        wo_mm[wslot] = tokM
                        ct.wait(v, tokM)
                        dOc[bsel].wait(v)
                        if p_i == 0:
                            tokE = vop(v.tensor_copy, o_bf[bsel][:, 0:dl],
                                       ps_op[pslot][:, 0:dl])
                        else:
                            dXc[bsel].wait(v)
                            tokE = vop(v.tensor_tensor, o_bf[bsel][:, 0:dl],
                                       ps_op[pslot][:, 0:dl],
                                       x1t[bsel][:, 0:dl], ALU.add)
                            x1_rd[bsel] = tokE
                        ps_tok[pslot] = tokE
                        cv.wait(sy, tokE)
                        sy.dma_start(
                            dst_sc[qsub * 128:(qsub + 1) * 128, do:do + dl],
                            o_bf[bsel][:, 0:dl]).then_inc(dOc[bsel].s, 16)
                        dOc[bsel].n += 16
                barrier()

        with ExitStack() as stF:
            sbF = lambda n, sh, d: stF.enter_context(nc.sbuf_tensor(n, sh, d))
            psF = lambda n, sh, d: stF.enter_context(nc.psum_tensor(n, sh, d))
            ln2T = sbF("ln2T", [128, DT * Q], BF16)
            ln2T3 = ln2T.ap().rearrange("p (d r) -> p d r", d=DT)
            ptrb = [psF(f"ptrbF{i}", [128, 8 * 128], BF16) for i in range(2)]
            with ExitStack() as stF1:
                sbF1 = lambda n, sh, d: stF1.enter_context(nc.sbuf_tensor(n, sh, d))
                xt = [sbF1(f"xtF{i}", [128, D], F32) for i in range(2)]
                yb = [sbF1(f"ybF{i}", [128, D], F32) for i in range(2)]
                sqj = sbF1("sqjF", [128, D], F32)
                lnbf = sbF1("lnbfF", [128, D], BF16)
                stats = [sbF1(f"stF{i}", [128, 1], F32) for i in range(5)]
                ptrb_tok = [0, 0]
                yw = [(0, 0), (0, 0)]
                xq3b = xq.rearrange("(a p) d -> a p d", p=128)
                dYs = [dch[4], dch[5]]
                for qsub in range(QT):
                    bs = qsub % 2
                    ch = dch[2 + bs]

                    def extra():
                        dYs[bs].wait(sy)
                        sy.dma_start(yb[bs][:, :],
                                     y_sc[qsub * 128:(qsub + 1) * 128, :]) \
                            .then_inc(ch.s, 16)
                        ch.n += 16

                    tok_add = [0]
                    cv.wait(sy, yw[bs][0])
                    cs.wait(sy, yw[bs][1])
                    sy.dma_start(xt[bs][:, :], xq3b[qsub, :, :]) \
                        .then_inc(ch.s, 16)
                    ch.n += 16
                    extra()
                    ch.wait(v)
                    toky = vop(v.tensor_tensor, yb[bs][:, :], yb[bs][:, :],
                               xt[bs][:, :], ALU.add)
                    cv.wait(sy, toky)
                    sy.dma_start(y_sc[qsub * 128:(qsub + 1) * 128, :],
                                 yb[bs][:, :]).then_inc(dYs[bs].s, 16)
                    dYs[bs].n += 16
                    nsum, ssq, m2, varv, rstd = stats
                    cv.wait(s, toky)
                    s.wait_ge(cv.s, cv.n)
                    toksq = sop(s.activation, sqj[:, :], yb[bs][:, :], AF.Square,
                                accum_out=ssq[:, :])
                    vop(v.tensor_reduce, nsum[:, :], yb[bs][:, :], AX.X, ALU.add)
                    vop(v.tensor_scalar_mul, nsum[:, :], nsum[:, :], -1.0 / D)
                    vop(v.tensor_tensor, m2[:, :], nsum[:, :], nsum[:, :],
                        ALU.mult)
                    cs.wait(v, toksq)
                    vop(v.tensor_scalar_mul, varv[:, :], ssq[:, :], 1.0 / D)
                    tokv = vop(v.tensor_tensor, varv[:, :], varv[:, :], m2[:, :],
                               ALU.subtract)
                    cv.wait(s, tokv)
                    sop(s.activation, varv[:, :], varv[:, :], AF.Ln, bias=cb[:, 0:1])
                    toks = sop(s.activation, rstd[:, :], varv[:, :], AF.Exp,
                               scale=cb[:, 1:2])
                    cs.wait(v, toks)
                    tokl = vop(v.tensor_scalar, lnbf[:, :], yb[bs][:, :],
                               nsum[:, 0:1], rstd[:, 0:1], ALU.add, ALU.mult)
                    yw[bs] = (tokl, toksq)
                    transpose_to(ln2T3[:, :, qsub * 128:(qsub + 1) * 128], DT,
                                 lambda dd: lnbf[:, dd * 128:(dd + 1) * 128],
                                 ptrb, ptrb_tok, tokl)
                barrier()

            with ExitStack() as stF2:
                sbF2 = lambda n, sh, d: stF2.enter_context(nc.sbuf_tensor(n, sh, d))
                pkb = [sbF2(f"pkb{i}", [128, DR * 64], I32) for i in range(4)]
                sclb = [sbF2(f"sclb{i}", [128, 2 * DR], F32) for i in range(4)]
                tmpd = [sbF2(f"tmpd{i}", [128, DR * 128], I32) for i in range(2)]
                wdq = [sbF2(f"wdq{i}", [128, D], BF16) for i in range(4)]
                wT = [sbF2(f"wTf{i}", [128, DT * 128], BF16) for i in range(4)]
                wT3 = [w.ap().rearrange("p (d o) -> p d o", d=DT) for w in wT]
                sab = [sbF2(f"sab{i}", [128, Q], BF16) for i in range(2)]
                hab = [sbF2(f"hab{i}", [128, Q], BF16) for i in range(2)]
                ps_ab = [psF(f"psab{i}", [128, Q], F32) for i in range(2)]

                pk1_r = pk1.rearrange("(h r) t -> h (r t)", r=DR)
                pk3_r = pk3.rearrange("(h r) t -> h (r t)", r=DR)
                s1_r = s1.rearrange("(h g) -> h g", g=2 * DR)
                s3_r = s3.rearrange("(h g) -> h g", g=2 * DR)
                qsp = _splits(Q, 512)

                dPK = [dch[21], dch[22], dch[19], dch[20]]
                deq_rd = [0, 0, 0, 0]
                pk_rd = [0, 0, 0, 0]
                wT_mm = [0, 0, 0, 0]
                silu_last = 0
                tokh_slot = [0, 0]
                tokh_last = 0
                dHA = [dch[23], dch[3]]

                def load_pk(buf_i, pk_r, s_r, row_t, col0, nh):
                    ch = dPK[buf_i]
                    cv.wait(sy, pk_rd[buf_i])
                    sy.dma_start(pkb[buf_i][:, 0:nh * 64],
                                 pk_r[row_t * 128:(row_t + 1) * 128,
                                      col0 * 64:(col0 + nh) * 64]) \
                        .then_inc(ch.s, 16)
                    sy.dma_start(sclb[buf_i][:, 0:2 * nh],
                                 s_r[row_t * 128:(row_t + 1) * 128,
                                     2 * col0:2 * (col0 + nh)]) \
                        .then_inc(ch.s, 16)
                    ch.n += 32

                def dequant(buf_i, tsl, nh, FSCL):
                    ch = dPK[buf_i]
                    ct.wait(v, deq_rd[buf_i])
                    ch.wait(v)
                    pk3v = pkb[buf_i].ap() \
                        .rearrange("p (r t) -> p r t", t=64)[:, 0:nh, :]
                    td = tmpd[tsl].ap() \
                        .rearrange("p (r h t) -> p r h t", h=2, t=64)[:, 0:nh]
                    vop(v.tensor_single_scalar, td[:, :, 0, :], pk3v, 4,
                        ALU.arith_shift_right)
                    vop(v.tensor_scalar, td[:, :, 1, :], pk3v, 28, 28,
                        ALU.logical_shift_left, ALU.arith_shift_right)
                    wv = wdq[buf_i].ap() \
                        .rearrange("p (r h t) -> p r h t", h=2, t=64)[:, 0:nh]
                    sc_m = bass.AP(sclb[buf_i].ap().tensor, 0,
                                   [[FSCL, 128], [2, nh], [0, 64]])
                    sc_l = bass.AP(sclb[buf_i].ap().tensor, 1,
                                   [[FSCL, 128], [2, nh], [0, 64]])
                    vop(v.tensor_tensor, wv[:, :, 0, :], td[:, :, 0, :], sc_m,
                        ALU.mult)
                    tokd = vop(v.tensor_tensor, wv[:, :, 1, :], td[:, :, 1, :],
                               sc_l, ALU.mult)
                    return tokd

                srcs = [(pk1_r, s1_r), (pk3_r, s3_r)]

                def deq_tr(ht):
                    for wi in range(2):
                        bi = 2 * (ht % 2) + wi
                        tokd = dequant(bi, wi, DR, 2 * DR)
                        pk_rd[bi] = tokd
                        ct.wait(t, wT_mm[bi])
                        transpose_to(wT3[bi], DT,
                                     lambda dd: wdq[bi][:, dd * 128:(dd + 1) * 128],
                                     ptrb, ptrb_tok, tokd)
                        deq_rd[bi] = ct.n

                for wi in range(2):
                    load_pk(2 * 0 + wi, *srcs[wi], 0, 0, DR)
                if NHT > 1:
                    for wi in range(2):
                        load_pk(2 * 1 + wi, *srcs[wi], 1, 0, DR)
                deq_tr(0)
                for ht in range(NHT):
                    for wi in range(2):
                        bi = 2 * (ht % 2) + wi
                        cv.wait(t, ptrb_tok[0])
                        cv.wait(t, ptrb_tok[1])
                        if wi == 0:
                            cs.wait(t, silu_last)
                        else:
                            cv.wait(t, tokh_last)
                        for no, nl in qsp:
                            for dd in range(DT):
                                ins = t.matmul(ps_ab[wi][:, no:no + nl],
                                               wT3[bi][:, dd, :],
                                               ln2T3[:, dd, no:no + nl],
                                               start=(dd == 0),
                                               stop=(dd == DT - 1))
                        wT_mm[bi] = ct.inc(ins)
                    if ht + 2 < NHT:
                        for wi in range(2):
                            load_pk(2 * ((ht + 2) % 2) + wi, *srcs[wi],
                                    ht + 2, 0, DR)
                    if ht + 1 < NHT:
                        deq_tr(ht + 1)
                    hsl = ht % 2
                    ct.wait(s, wT_mm[2 * hsl + 0])
                    dHA[hsl].wait(s)
                    cv.wait(s, tokh_slot[hsl])
                    silu_last = sop(s.activation, sab[hsl][:, :],
                                    ps_ab[0][:, :], AF.Silu)
                    cs.wait(v, silu_last)
                    ct.wait(v, wT_mm[2 * hsl + 1])
                    dHA[hsl].wait(v)
                    tokh = vop(v.tensor_tensor, hab[hsl][:, :], sab[hsl][:, :],
                               ps_ab[1][:, :], ALU.mult)
                    tokh_slot[hsl] = tokh
                    tokh_last = tokh
                    cv.wait(sy, tokh)
                    sy.dma_start(hact_sc[ht * 128:(ht + 1) * 128, :],
                                 hab[hsl][:, :]).then_inc(dHA[hsl].s, 16)
                    dHA[hsl].n += 16
                barrier()

        with ExitStack() as stG:
            sbG = lambda n, sh, d: stG.enter_context(nc.sbuf_tensor(n, sh, d))
            psG = lambda n, sh, d: stG.enter_context(nc.psum_tensor(n, sh, d))
            HH = HR // 2 if HR % 2 == 0 else HR
            nhalves = HR // HH
            w2T = sbG("w2T", [128, HR * 512], BF16)
            w2T3 = w2T.ap().rearrange("p (h d) -> p h d", h=HR)
            HB = 4
            hbuf = [sbG(f"hbuf{i}", [128, HB * Q], BF16) for i in range(2)]
            QG = min(4, QT)
            ytl = [sbG(f"ytl{i}", [128, 512], F32) for i in range(4)]
            otl = [sbG(f"otl{i}", [128, 512], F32) for i in range(4)]
            pkb = [sbG(f"pkbG{i}", [128, HH * 64], I32) for i in range(2)]
            sclb = [sbG(f"sclbG{i}", [128, 2 * HH], F32) for i in range(2)]
            tmpd = [sbG("tmpdG", [128, HH * 128], I32)]
            wdq = [sbG(f"wdqG{i}", [128, HH * 128], BF16) for i in range(2)]
            ptrb = [psG(f"ptrbG{i}", [128, 8 * 128], BF16) for i in range(2)]
            ps_q = [psG(f"psq{i}", [128, 512], F32) for i in range(QG)]

            pk2_r = pk2.rearrange("(d r) t -> d (r t)", r=HR)
            s2_r = s2.rearrange("(d g) -> d g", g=2 * HR)
            dPK = [dch[21], dch[22]]
            deq_rd = [0, 0]
            ptrb_tok = [0, 0]

            def dequant2(buf_i, row_t, col0, nh):
                ch = dPK[buf_i]
                ct.wait(sy, deq_rd[buf_i])
                sy.dma_start(pkb[buf_i][:, 0:nh * 64],
                             pk2_r[row_t * 128:(row_t + 1) * 128,
                                   col0 * 64:(col0 + nh) * 64]).then_inc(ch.s, 16)
                sy.dma_start(sclb[buf_i][:, 0:2 * nh],
                             s2_r[row_t * 128:(row_t + 1) * 128,
                                  2 * col0:2 * (col0 + nh)]).then_inc(ch.s, 16)
                ch.n += 32
                ch.wait(v)
                pk3v = pkb[buf_i].ap().rearrange("p (r t) -> p r t", t=64)[:, 0:nh]
                td = tmpd[0].ap().rearrange("p (r h t) -> p r h t",
                                            h=2, t=64)[:, 0:nh]
                vop(v.tensor_single_scalar, td[:, :, 0, :], pk3v, 4,
                    ALU.arith_shift_right)
                vop(v.tensor_scalar, td[:, :, 1, :], pk3v, 28, 28,
                    ALU.logical_shift_left, ALU.arith_shift_right)
                wv = wdq[buf_i].ap().rearrange("p (r h t) -> p r h t",
                                               h=2, t=64)[:, 0:nh]
                FS = 2 * HH
                sc_m = bass.AP(sclb[buf_i].ap().tensor, 0,
                               [[FS, 128], [2, nh], [0, 64]])
                sc_l = bass.AP(sclb[buf_i].ap().tensor, 1,
                               [[FS, 128], [2, nh], [0, 64]])
                vop(v.tensor_tensor, wv[:, :, 0, :], td[:, :, 0, :], sc_m,
                    ALU.mult)
                tokd = vop(v.tensor_tensor, wv[:, :, 1, :], td[:, :, 1, :],
                           sc_l, ALU.mult)
                return tokd

            dH2 = [dch[12], gch[1]]
            dY2 = [dch[14], dch[15], dch[18], dch[23]]
            dO2 = [dch[16], dch[17], dch[8], dch[9]]
            dsp = _splits(D, 512)

            def emit_deq_tr(di, dl):
                ndd = dl // 128
                for i in range(ndd):
                    ddg = di * (512 // 128) + i
                    for half in range(nhalves):
                        bi = (i * nhalves + half) % 2
                        tokd = dequant2(bi, ddg, half * HH, HH)
                        transpose_to(
                            w2T3[:, half * HH:(half + 1) * HH,
                                 i * 128:(i + 1) * 128],
                            HH,
                            lambda hh: wdq[bi][:, hh * 128:(hh + 1) * 128],
                            ptrb, ptrb_tok, tokd)
                        deq_rd[bi] = ct.n
                return cv.n

            hact_r = hact_sc.rearrange("(b p) q -> b p q", p=128)
            h_mm = [0, 0]
            evac_last = 0
            y_rd = [0, 0, 0, 0]
            nhb = (NHT + HB - 1) // HB
            w2_ready = emit_deq_tr(0, dsp[0][1])

            def load_hb(hbi):
                slot = hbi % 2
                eng = sy if slot == 0 else g
                ct.wait(eng, h_mm[slot])
                nb = min(HB, NHT - hbi * HB)
                eng.dma_start(
                    hbuf[slot].ap().rearrange("p (j q) -> p j q", q=Q)[:, 0:nb, :],
                    hact_r[hbi * HB:hbi * HB + nb, :, :].transpose([1, 0, 2])) \
                    .then_inc(dH2[slot].s, 16)
                dH2[slot].n += 16

            for di, (do, dl) in enumerate(dsp):
                for q0 in range(0, QT, QG):
                    qn = min(QG, QT - q0)
                    for qi in range(qn):
                        cv.wait(sy, y_rd[qi])
                        sy.dma_start(
                            ytl[qi][:, 0:dl],
                            y_sc[(q0 + qi) * 128:(q0 + qi + 1) * 128,
                                 do:do + dl]).then_inc(dY2[qi].s, 16)
                        dY2[qi].n += 16
                    load_hb(0)
                    for hbi in range(nhb):
                        if hbi + 1 < nhb:
                            load_hb(hbi + 1)
                        slot = hbi % 2
                        dH2[slot].wait(t)
                        if hbi == 0:
                            t.wait_ge(cv.s, w2_ready)
                            cv.wait(t, evac_last)
                        hb3 = hbuf[slot].ap() \
                            .rearrange("p (j q) -> p j q", q=Q)
                        for j in range(min(HB, NHT - hbi * HB)):
                            hh = hbi * HB + j
                            for qi in range(qn):
                                ins = t.matmul(
                                    ps_q[qi][:, 0:dl],
                                    hb3[:, j, (q0 + qi) * 128:(q0 + qi + 1) * 128],
                                    w2T3[:, hh, 0:dl],
                                    start=(hh == 0), stop=(hh == HR - 1))
                        h_mm[slot] = ct.inc(ins)
                    tokM = h_mm[(nhb - 1) % 2]
                    for qi in range(qn):
                        ct.wait(v, tokM)
                        dY2[qi].wait(v)
                        dO2[qi].wait(v)
                        tokO = vop(v.tensor_tensor, otl[qi][:, 0:dl],
                                   ps_q[qi][:, 0:dl], ytl[qi][:, 0:dl], ALU.add)
                        y_rd[qi] = tokO
                        evac_last = tokO
                        cv.wait(sy, tokO)
                        sy.dma_start(
                            out_d[(q0 + qi) * 128:(q0 + qi + 1) * 128,
                                  do:do + dl],
                            otl[qi][:, 0:dl]).then_inc(dO2[qi].s, 16)
                        dO2[qi].n += 16
                if di + 1 < len(dsp):
                    w2_ready = emit_deq_tr(di + 1, dsp[di + 1][1])
            for ch_ in dch:
                if ch_.n:
                    ch_.wait(sy)
            for ch_ in gch:
                if ch_.n:
                    ch_.wait(g)
    return nc


def _rope_tables(n, qoff=0):
    hd = 128
    inv = 1.0 / (THETA ** (np.arange(0, hd, 2, dtype=np.float64) / hd))
    pos = np.arange(qoff, qoff + n, dtype=np.float64)
    fr = np.outer(pos, inv)
    emb = np.concatenate([fr, fr], axis=1)
    cos = np.cos(emb).T.astype(np.float32)
    sinm = np.sin(emb).T.astype(np.float32)
    sinm[:64] *= -1.0
    return np.ascontiguousarray(cos), np.ascontiguousarray(sinm)


def make_core_inputs(c: Cfg, core, x, wdict):
    b = core // c.cpb
    j = core % c.cpb
    qoff = j * c.Q
    cos, sinm = _rope_tables(c.S, 0)
    cosq, sinmq = _rope_tables(c.Q, qoff)
    mask = (np.arange(c.S)[:, None] <= (qoff + np.arange(c.Q))[None, :])
    d = {
        "x_full": np.ascontiguousarray(x[b]),
        "x_q": np.ascontiguousarray(x[b, qoff:qoff + c.Q]),
        "cosk": cos, "sinmk": sinm, "cosq": cosq, "sinmq": sinmq,
        "maskT": mask.astype(ml_dtypes.bfloat16),
        "identb": np.eye(128, dtype=ml_dtypes.bfloat16),
        "onesf": np.ones((1, 128), dtype=np.float32),
        "onesb": np.ones((128, 1), dtype=ml_dtypes.bfloat16),
        "constv": np.tile(np.array([[EPS, -0.5, c.SCALE, -CEXP]], np.float32),
                          (128, 1)),
    }
    d.update(wdict)
    return d


def make_wdict(L0_attn_w_q, L0_attn_w_k, L0_attn_w_v, L0_attn_w_o, L0_attn_w_lq,
               L0_attn_w_lk, L0_attn_w_lv, L0_attn_w_lo, L0_ffn_w1, L0_ffn_s1,
               L0_ffn_w2, L0_ffn_s2, L0_ffn_w3, L0_ffn_s3):
    return {
        "w_q": np.ascontiguousarray(np.asarray(L0_attn_w_q, np.float32)),
        "w_k": np.ascontiguousarray(np.asarray(L0_attn_w_k, np.float32)),
        "w_v": np.ascontiguousarray(np.asarray(L0_attn_w_v, np.float32)),
        "w_o": np.ascontiguousarray(np.asarray(L0_attn_w_o, np.float32)),
        "w_lq": np.ascontiguousarray(np.asarray(L0_attn_w_lq, np.float32)),
        "w_lk": np.ascontiguousarray(np.asarray(L0_attn_w_lk, np.float32)),
        "w_lv": np.ascontiguousarray(np.asarray(L0_attn_w_lv, np.float32)),
        "w_lo": np.ascontiguousarray(np.asarray(L0_attn_w_lo, np.float32)),
        "pk1": np.ascontiguousarray(np.asarray(L0_ffn_w1, np.int32)),
        "s1": np.ascontiguousarray(np.asarray(L0_ffn_s1, np.float32)),
        "pk2": np.ascontiguousarray(np.asarray(L0_ffn_w2, np.int32)),
        "s2": np.ascontiguousarray(np.asarray(L0_ffn_s2, np.float32)),
        "pk3": np.ascontiguousarray(np.asarray(L0_ffn_w3, np.int32)),
        "s3": np.ascontiguousarray(np.asarray(L0_ffn_s3, np.float32)),
    }


def kernel(L0_attn_hidden_states, L0_attn_w_q, L0_attn_w_k, L0_attn_w_v,
           L0_attn_w_o, L0_attn_w_lq, L0_attn_w_lk, L0_attn_w_lv, L0_attn_w_lo,
           L0_ffn_w1, L0_ffn_s1, L0_ffn_w2, L0_ffn_s2, L0_ffn_w3, L0_ffn_s3):
    c = Cfg()
    x = np.asarray(L0_attn_hidden_states, dtype=np.float32)
    wdict = make_wdict(L0_attn_w_q, L0_attn_w_k, L0_attn_w_v, L0_attn_w_o,
                       L0_attn_w_lq, L0_attn_w_lk, L0_attn_w_lv, L0_attn_w_lo,
                       L0_ffn_w1, L0_ffn_s1, L0_ffn_w2, L0_ffn_s2, L0_ffn_w3,
                       L0_ffn_s3)
    nc = build_program(c)
    in_maps = [make_core_inputs(c, core, x, wdict) for core in range(c.ncores)]
    res = run_bass_kernel_spmd(nc, in_maps, core_ids=list(range(c.ncores)))
    out = np.empty((c.B, c.S, c.D), dtype=np.float32)
    for core in range(c.ncores):
        b, j = core // c.cpb, core % c.cpb
        out[b, j * c.Q:(j + 1) * c.Q] = res.results[core]["out"]
    return out
